# version 05
# baseline (speedup 1.0000x reference)
import numpy as np
import ml_dtypes
import concourse.bass as bass
import concourse.mybir as mybir
from concourse.bass_utils import run_bass_kernel_spmd

F32 = mybir.dt.float32
BF16 = mybir.dt.bfloat16
AF = mybir.ActivationFunctionType
ALU = mybir.AluOpType
bfdt = ml_dtypes.bfloat16

B, T, C = 16, 2048, 1024
H, D = 16, 64
HID = 4096
NCORE = 8
SEQ = 2
TILE = 512
NT = T // TILE
CH = C // 128
HCH = HID // 128
KT = T // 128
EPS = 1e-5

_CACHED = {}



def _pack_weights(inp):
    w = {}

    def cat(p1, p2):
        a = np.asarray(p1).reshape(H, 64, C)
        b = np.asarray(p2).reshape(H, 64, C)
        return np.stack([a, b], axis=1).reshape(2 * C, C)

    def to_sb(mat, och):
        t = np.asarray(mat).T.reshape(CH, 128, och * 128).transpose(1, 0, 2)
        return np.ascontiguousarray(t).astype(np.float32)

    w["wkcat"] = to_sb(cat(inp["L0_attn_Wk1"], inp["L0_attn_Wk2"]), 16)
    w["wqcat"] = to_sb(cat(inp["L0_attn_Wq1"], inp["L0_attn_Wq2"]), 16)
    w["wv"] = to_sb(inp["L0_attn_Wv"], 8)
    w["wout"] = to_sb(inp["L0_attn_Wout"], 8)

    def ffn_chunks(mat):
        mat = np.asarray(mat)
        o, i = mat.shape
        noc, nic = o // 128, i // 128
        t = mat.T.reshape(nic, 128, noc, 128).transpose(2, 1, 0, 3)
        return np.ascontiguousarray(t).astype(np.float32)

    def ffn_groups(mat):
        mat = np.asarray(mat)
        t = mat.T.reshape(CH, 128, 8, 512).transpose(2, 1, 0, 3)
        return np.ascontiguousarray(t).astype(np.float32)

    w["w1src"] = ffn_groups(inp["L0_ffn_w1_w"])
    w["w2src"] = ffn_groups(inp["L0_ffn_w2_w"])
    w["h1src"] = ffn_chunks(inp["L0_ffn_h1_w"])
    w["h2src"] = ffn_chunks(inp["L0_ffn_h2_w"])
    w["w3src"] = ffn_chunks(inp["L0_ffn_w3_w"])

    def scl(v):
        return np.ascontiguousarray(np.asarray(v).reshape(-1, 128).T).astype(
            np.float32)

    w["s1v"] = scl(inp["L0_ffn_w1_s"])
    w["s2v"] = scl(inp["L0_ffn_w2_s"])
    w["s3v"] = scl(inp["L0_ffn_h1_s"])
    w["s4v"] = scl(inp["L0_ffn_h2_s"])
    w["s5v"] = scl(inp["L0_ffn_w3_s"])
    return w


def _host_consts():
    c = {}
    inv = 1.0 / (10000.0 ** (np.arange(0, D, 2, dtype=np.float32) / D))
    freqs = np.arange(T, dtype=np.float32)[:, None] * inv[None, :]
    cos = np.cos(freqs).astype(bfdt).astype(np.float32).T
    sin = np.sin(freqs).astype(bfdt).astype(np.float32).T
    c["ctab"] = np.ascontiguousarray(np.tile(cos, (4, 1)))
    c["stab"] = np.ascontiguousarray(
        np.concatenate([-sin, sin, -sin, sin], axis=0))
    m = np.arange(128)
    swp = np.where((m % 64) < 32, m + 32, m - 32)
    S = np.zeros((128, 128), np.float32)
    S[swp, m] = 1.0
    c["swapm"] = S
    c["onesm"] = np.full((128, 128), 1.0 / C, np.float32)
    return c


def _pack_x(x):
    out = []
    for cid in range(NCORE):
        xc = x[2 * cid:2 * cid + 2]
        t = xc.transpose(0, 2, 1).reshape(SEQ, CH, 128, T).transpose(0, 2, 1, 3)
        out.append(np.ascontiguousarray(t).astype(np.float32))
    return out


def _unpack_y(parts):
    outs = []
    for arr in parts:
        a = arr.reshape(128, CH, SEQ, T).transpose(2, 3, 1, 0)
        outs.append(a.reshape(SEQ, T, C))
    return np.ascontiguousarray(np.concatenate(outs, axis=0))



class _Sem:
    __slots__ = ("h", "n", "name", "dma")

    def __init__(self, h, name, dma=False):
        self.h = h
        self.n = 0
        self.name = name
        self.dma = dma


class _Prog:
    def __init__(self):
        self.events = []
        self.writes = {}
        self.guards = {}
        self.eng_ctr = {}

    def ev(self, eng, fn, reads=(), writes=(), extra=(), dma_sem=None, note=""):
        waits = {}

        def add(pair):
            if pair is None:
                return
            s, v = pair
            if v <= 0:
                return
            if waits.get(s, 0) < v:
                waits[s] = v

        for r in reads:
            add(self.writes.get(r))
        for r in writes:
            for p in self.guards.get(r, ()):
                add(p)
            add(self.writes.get(r))
        for p in extra:
            add(p)

        if dma_sem is not None:
            dma_sem.n += 16
            inc = (dma_sem, dma_sem.n)
        else:
            ctr = self.eng_ctr[eng]
            ctr.n += 1
            inc = (ctr, ctr.n)

        own = self.eng_ctr.get(eng)
        wl = [(s, v) for s, v in waits.items() if s is not own]

        self.events.append(dict(eng=eng, fn=fn, waits=wl, inc=inc, note=note))
        for r in reads:
            self.guards.setdefault(r, []).append(inc)
        for r in writes:
            self.writes[r] = inc
            self.guards[r] = []
        return inc

    def check(self):
        eng_names = sorted({e["eng"] for e in self.events})
        queues = {n: [e for e in self.events if e["eng"] == n]
                  for n in eng_names}
        ptr = {n: 0 for n in eng_names}
        val = {}
        total = len(self.events)
        done = 0
        progress = True
        while progress:
            progress = False
            for n in eng_names:
                while ptr[n] < len(queues[n]):
                    e = queues[n][ptr[n]]
                    if all(val.get(s, 0) >= v for s, v in e["waits"]):
                        s, v = e["inc"]
                        val[s] = max(val.get(s, 0), v)
                        ptr[n] += 1
                        done += 1
                        progress = True
                    else:
                        break
        if done != total:
            lines = []
            for n in eng_names:
                if ptr[n] < len(queues[n]):
                    e = queues[n][ptr[n]]
                    missing = [(s.name, v, val.get(s, 0))
                               for s, v in e["waits"] if val.get(s, 0) < v]
                    lines.append(f"  {n}@{ptr[n]}/{len(queues[n])} "
                                 f"'{e['note']}' missing {missing}")
            raise RuntimeError("schedule deadlock:\n" + "\n".join(lines))



def _build(dbg=False):
    from contextlib import ExitStack

    nc = bass.Bass("TRN2", target_bir_lowering=False, debug=False)
    P = _Prog()

    dx = nc.dram_tensor("x", [SEQ, 128, CH, T], F32, kind="ExternalInput").ap()
    dwk = nc.dram_tensor("wkcat", [128, CH, 2048], F32, kind="ExternalInput").ap()
    dwq = nc.dram_tensor("wqcat", [128, CH, 2048], F32, kind="ExternalInput").ap()
    dwv = nc.dram_tensor("wv", [128, CH, 1024], F32, kind="ExternalInput").ap()
    dwo = nc.dram_tensor("wout", [128, CH, 1024], F32, kind="ExternalInput").ap()
    dct = nc.dram_tensor("ctab", [128, T], F32, kind="ExternalInput").ap()
    dstb = nc.dram_tensor("stab", [128, T], F32, kind="ExternalInput").ap()
    dsw = nc.dram_tensor("swapm", [128, 128], F32, kind="ExternalInput").ap()
    don = nc.dram_tensor("onesm", [128, 128], F32, kind="ExternalInput").ap()
    dw1 = nc.dram_tensor("w1src", [8, 128, CH, 512], F32, kind="ExternalInput").ap()
    dw2 = nc.dram_tensor("w2src", [8, 128, CH, 512], F32, kind="ExternalInput").ap()
    dh1 = nc.dram_tensor("h1src", [HCH, 128, HCH, 128], F32, kind="ExternalInput").ap()
    dh2 = nc.dram_tensor("h2src", [HCH, 128, HCH, 128], F32, kind="ExternalInput").ap()
    dw3 = nc.dram_tensor("w3src", [CH, 128, HCH, 128], F32, kind="ExternalInput").ap()
    dscl = {}
    for nme, cols in (("s1v", HCH), ("s2v", HCH), ("s3v", HCH), ("s4v", HCH),
                      ("s5v", CH)):
        dscl[nme] = nc.dram_tensor(nme, [128, cols], F32,
                                   kind="ExternalInput").ap()
    dy = nc.dram_tensor("y", [128, CH, SEQ * T], F32, kind="ExternalOutput").ap()

    sp_w1 = nc.dram_tensor("sp_w1", [8, 128, CH, 512], BF16, kind="Internal").ap()
    sp_w2 = nc.dram_tensor("sp_w2", [8, 128, CH, 512], BF16, kind="Internal").ap()
    sp_h1 = nc.dram_tensor("sp_h1", [HCH, 128, HCH, 128], BF16, kind="Internal").ap()
    sp_h2 = nc.dram_tensor("sp_h2", [HCH, 128, HCH, 128], BF16, kind="Internal").ap()
    sp_w3 = nc.dram_tensor("sp_w3", [CH, 128, HCH, 128], BF16, kind="Internal").ap()
    kdbg = "ExternalOutput" if dbg else "Internal"
    sp_k = nc.dram_tensor("sp_k", [KT, 2, 128, 8, 128], BF16, kind=kdbg).ap()
    sp_v = nc.dram_tensor("sp_v", [KT, 128, 1024], BF16, kind=kdbg).ap()
    sp_x2 = nc.dram_tensor("sp_x2", [SEQ, 128, CH, T], F32,
                           kind="ExternalOutput" if dbg else "Internal").ap()

    stack = ExitStack()
    E = stack.enter_context

    def sem(name, dma=False):
        return _Sem(E(nc.semaphore(name)), name, dma)

    SW = sem("SW", True)
    SC = sem("SC", True)
    SPL = sem("SPL", True)
    SX = sem("SX", True)
    SKK = sem("SKK", True)
    SKV = [sem("SKV0", True), sem("SKV1", True)]
    KSL = [sem("KSL0", True), sem("KSL1", True)]
    VSL = [sem("VSL0", True), sem("VSL1", True)]
    HSL = [sem("HSL0", True), sem("HSL1", True)]
    WSL = [sem(f"WSL{i}", True) for i in range(6)]
    SX2 = sem("SX2", True)
    SO = sem("SO", True)
    CT = sem("CT")
    CS = sem("CS")
    CV = sem("CV")
    CG = sem("CG")
    P.eng_ctr = {"tensor": CT, "scalar": CS, "vector": CV, "gpsimd": CG}

    awt = E(nc.sbuf_tensor("awt", [128, CH, 3072], BF16))
    x_s = [E(nc.sbuf_tensor(f"x_s{i}", [128, CH, TILE], F32)) for i in range(2)]
    xn_s = [E(nc.sbuf_tensor(f"xn_s{i}", [128, CH, TILE], BF16))
            for i in range(2)]
    cbf = [E(nc.sbuf_tensor(f"cbf{i}", [128, TILE], BF16)) for i in range(2)]
    csq = [E(nc.sbuf_tensor(f"csq{i}", [128, TILE], BF16)) for i in range(2)]
    mean_s = E(nc.sbuf_tensor("mean_s", [128, TILE], F32))
    rstd_s = E(nc.sbuf_tensor("rstd_s", [128, TILE], F32))
    vtmp = E(nc.sbuf_tensor("vtmp", [128, TILE], F32))
    t1_s = E(nc.sbuf_tensor("t1_s", [128, TILE], F32))
    out_s = E(nc.sbuf_tensor("out_s", [128, CH, TILE], F32))
    swap_s = E(nc.sbuf_tensor("swap_s", [128, 128], BF16))
    ones_s = E(nc.sbuf_tensor("ones_s", [128, 128], BF16))
    epsb = E(nc.sbuf_tensor("epsb", [128, 1], F32))
    scl_s = {}
    for nme, cols in (("s1v", HCH), ("s2v", HCH), ("s3v", HCH), ("s4v", HCH),
                      ("s5v", CH)):
        scl_s[nme] = E(nc.sbuf_tensor("sb_" + nme, [128, cols], F32))

    a_stack = ExitStack()
    A_ = a_stack.enter_context
    t2_s = A_(nc.sbuf_tensor("t2_s", [128, TILE], BF16))
    ctab_s = A_(nc.sbuf_tensor("ctab_s", [128, T], F32))
    stab_s = A_(nc.sbuf_tensor("stab_s", [128, T], F32))
    k12st = A_(nc.sbuf_tensor("k12st", [128, 16, TILE], BF16))
    vstage = [A_(nc.sbuf_tensor(f"vstage{i}", [128, 1024], BF16))
              for i in range(2)]
    q12_s = A_(nc.sbuf_tensor("q12_s", [128, 16, TILE], BF16))
    kslab = [A_(nc.sbuf_tensor(f"kslab{i}", [128, 8, 128], BF16))
             for i in range(2)]
    vslab = [A_(nc.sbuf_tensor(f"vslab{i}", [128, TILE], BF16))
             for i in range(2)]
    s1s = A_(nc.sbuf_tensor("s1s", [128, TILE], F32))
    p_s = A_(nc.sbuf_tensor("p_s", [128, 2, TILE], BF16))
    yfm = A_(nc.sbuf_tensor("yfm", [128, CH, TILE], BF16))

    pair0 = E(nc.psum_tensor("pair0", [128, 1024], F32))
    pair1 = E(nc.psum_tensor("pair1", [128, 1024], F32))
    pbank = [E(nc.psum_tensor(f"pb{i}", [128, 512], F32)) for i in range(4)]

    def dma(eng, out_ap, in_ap, s, reads=(), writes=(), extra=(), note=""):
        def fn(e, o=out_ap, i=in_ap):
            return e.dma_start(out=o, in_=i)
        return P.ev(eng, fn, reads=reads, writes=writes, extra=extra,
                    dma_sem=s, note=note)

    def mm_group(pairs, out_ap, reads, writes, extra=(), note="", start=True,
                 stop=True, skip=False):
        def fn(t, pr=tuple(pairs), o=out_ap, st=start, sp=stop, sk=skip):
            ins = None
            n = len(pr)
            for i, (l, r) in enumerate(pr):
                ins = t.matmul(o, l, r, start=(st and i == 0),
                               stop=(sp and i == n - 1), skip_group_check=sk)
            return ins
        return P.ev("tensor", fn, reads=reads, writes=writes, extra=extra,
                    note=note)

    def act(out_ap, in_ap, func, reads, writes, scale=1.0, bias=0.0, extra=(),
            note=""):
        def fn(s, o=out_ap, i=in_ap, f=func, sc=scale, b=bias):
            return s.activation(o, i, f, scale=sc, bias=b)
        return P.ev("scalar", fn, reads=reads, writes=writes, extra=extra,
                    note=note)

    def tt(out_ap, a_ap, b_ap, op, reads, writes, extra=(), note=""):
        def fn(v, o=out_ap, a=a_ap, b=b_ap, p=op):
            return v.tensor_tensor(o, a, b, p)
        return P.ev("vector", fn, reads=reads, writes=writes, extra=extra,
                    note=note)

    def vcopy(out_ap, in_ap, reads, writes, extra=(), note=""):
        def fn(v, o=out_ap, i=in_ap):
            return v.tensor_copy(o, i)
        return P.ev("vector", fn, reads=reads, writes=writes, extra=extra,
                    note=note)

    def vrecip(out_ap, in_ap, reads, writes, note=""):
        def fn(v, o=out_ap, i=in_ap):
            return v.reciprocal(o, i)
        return P.ev("vector", fn, reads=reads, writes=writes, note=note)

    def emit_ln_stats(src, src_res, tag, extra_first=()):
        for c in range(CH):
            pp = c % 2
            vcopy(cbf[pp][:, :], src[:, c, :],
                  reads=[src_res], writes=[f"cbf{pp}"],
                  extra=extra_first if c == 0 else (),
                  note=f"{tag}:cast{c}")
            tt(csq[pp][:, :], cbf[pp][:, :], cbf[pp][:, :], ALU.mult,
               reads=[f"cbf{pp}"], writes=[f"csq{pp}"], note=f"{tag}:sq{c}")
            mm_group([(ones_s[:, :], cbf[pp][:, :])], pbank[0][:, :],
                     reads=[f"cbf{pp}", "ones_s"],
                     writes=["pb0"] if c == 0 else [],
                     extra=() if c == 0 else [P.writes.get("pb0")],
                     note=f"{tag}:mmm{c}", start=(c == 0), stop=(c == CH - 1))
            if c != 0:
                P.writes["pb0"] = P.events[-1]["inc"]
            mm_group([(ones_s[:, :], csq[pp][:, :])], pbank[1][:, :],
                     reads=[f"csq{pp}", "ones_s"],
                     writes=["pb1"] if c == 0 else [],
                     extra=() if c == 0 else [P.writes.get("pb1")],
                     note=f"{tag}:mms{c}", start=(c == 0), stop=(c == CH - 1))
            if c != 0:
                P.writes["pb1"] = P.events[-1]["inc"]
        act(mean_s[:, :], pbank[0][:, :], AF.Copy, reads=["pb0"],
            writes=["mean_s"], note=f"{tag}:mean_ev")
        tt(vtmp[:, :], mean_s[:, :], mean_s[:, :], ALU.mult,
           reads=["mean_s"], writes=["vtmp"], note=f"{tag}:mean2")
        tt(vtmp[:, :], pbank[1][:, :], vtmp[:, :], ALU.subtract,
           reads=["pb1"], writes=["vtmp"], note=f"{tag}:var")
        act(t1_s[:, :], vtmp[:, :], AF.Sqrt, scale=1.0, bias=epsb[:, 0:1],
            reads=["vtmp", "epsb"], writes=["t1_s"], note=f"{tag}:sqrt")
        vrecip(rstd_s[:, :], t1_s[:, :], reads=["t1_s"], writes=["rstd_s"],
               note=f"{tag}:recip")

    def emit_ln(src, src_res, xn, xn_fmt, tag, extra_first=()):
        emit_ln_stats(src, src_res, tag, extra_first)
        for c in range(CH):
            tt(t1_s[:, :], src[:, c, :], mean_s[:, :], ALU.subtract,
               reads=[src_res, "mean_s"], writes=["t1_s"], note=f"{tag}:sub{c}")
            tt(xn[:, c, :], t1_s[:, :], rstd_s[:, :], ALU.mult,
               reads=["t1_s", "rstd_s"], writes=[xn_fmt.format(c)],
               note=f"{tag}:nrm{c}")

    def emit_ln_residual(xr, xr_res, tag):
        emit_ln_stats(out_s, "out_s", tag)
        for c in range(CH):
            tt(t1_s[:, :], out_s[:, c, :], mean_s[:, :], ALU.subtract,
               reads=["out_s", "mean_s"], writes=["t1_s"], note=f"{tag}:sub{c}")
            tt(vtmp[:, :], t1_s[:, :], rstd_s[:, :], ALU.mult,
               reads=["t1_s", "rstd_s"], writes=["vtmp"], note=f"{tag}:nrm{c}")
            tt(out_s[:, c, :], vtmp[:, :], xr[:, c, :], ALU.add,
               reads=["vtmp", xr_res], writes=["out_s"], note=f"{tag}:res{c}")

    def emit_rope(dst, dst_res_fmt, xn, xn_fmt, ct_sl, st_sl, tag):
        def proj(c):
            pa = c % 2
            mm_group([(awt[:, ic, 128 * c:128 * (c + 1)], xn[:, ic, :])
                      for ic in range(CH)],
                     pbank[pa][:, :],
                     reads=["awt"] + [xn_fmt.format(ic) for ic in range(CH)],
                     writes=[f"pb{pa}"], note=f"{tag}:proj{c}")

        def tail(c):
            pa, pb = c % 2, 2 + c % 2
            tt(t1_s[:, :], pbank[pa][:, :], ct_sl, ALU.mult,
               reads=[f"pb{pa}", "ctab"], writes=["t1_s"], note=f"{tag}:ct{c}")
            tt(t2_s[:, :], pbank[pa][:, :], st_sl, ALU.mult,
               reads=[f"pb{pa}", "stab"], writes=["t2_s"], note=f"{tag}:st{c}")
            mm_group([(swap_s[:, :], t2_s[:, :])], pbank[pb][:, :],
                     reads=["t2_s", "swap_s"], writes=[f"pb{pb}"],
                     note=f"{tag}:swap{c}")
            tt(dst[:, c, :], pbank[pb][:, :], t1_s[:, :], ALU.add,
               reads=[f"pb{pb}", "t1_s"], writes=[dst_res_fmt.format(c)],
               note=f"{tag}:rot{c}")

        for c in range(17):
            if c < 16:
                proj(c)
            if c >= 1:
                tail(c - 1)

    def msfn(v, o=epsb[:, 0:1], cc=EPS):
        return v.memset(o, cc)
    P.ev("vector", msfn, writes=["epsb"], note="memset eps")

    const_names = []
    dma("sync", ctab_s[:, :], dct, SC, writes=["ctab"], note="ld ctab")
    const_names.append("ctab")
    dma("sync", stab_s[:, :], dstb, SC, writes=["stab"], note="ld stab")
    const_names.append("stab")
    for nme in scl_s:
        dma("sync", scl_s[nme][:, :], dscl[nme], SC, writes=[nme],
            note=f"ld {nme}")
        const_names.append(nme)
    dma("gpsimd", swap_s[:, :], dsw, SC, writes=["swap_s"], note="ld swap")
    const_names.append("swap_s")
    dma("gpsimd", ones_s[:, :], don, SC, writes=["ones_s"], note="ld ones")
    const_names.append("ones_s")
    for nme in const_names:
        P.writes[nme] = (SC, SC.n)

    spill_pieces = []
    for qq in range(2):
        spill_pieces.append((sp_w1[4 * qq:4 * (qq + 1)],
                             dw1[4 * qq:4 * (qq + 1)], f"w1.{qq}"))
        spill_pieces.append((sp_w2[4 * qq:4 * (qq + 1)],
                             dw2[4 * qq:4 * (qq + 1)], f"w2.{qq}"))
        spill_pieces.append((sp_w3[4 * qq:4 * (qq + 1)],
                             dw3[4 * qq:4 * (qq + 1)], f"w3.{qq}"))
    for qq in range(8):
        spill_pieces.append((sp_h1[4 * qq:4 * (qq + 1)],
                             dh1[4 * qq:4 * (qq + 1)], f"h1.{qq}"))
    for qq in range(8):
        spill_pieces.append((sp_h2[4 * qq:4 * (qq + 1)],
                             dh2[4 * qq:4 * (qq + 1)], f"h2.{qq}"))
    spill_i = [0]

    def pace_spill(k=2):
        for _ in range(k):
            if spill_i[0] >= len(spill_pieces):
                return
            d2, s2_, nm2 = spill_pieces[spill_i[0]]
            spill_i[0] += 1
            dma("gpsimd", d2, s2_, SPL, extra=[(CT, CT.n)],
                note=f"spill {nm2}")

    def finish_spills():
        while spill_i[0] < len(spill_pieces):
            pace_spill(4)
        for nme in ("sp_w1", "sp_w2", "sp_w3", "sp_h1", "sp_h2"):
            P.writes[nme] = (SPL, SPL.n)

    gt = [0]

    for s in range(SEQ):
        dma("gpsimd", awt[:, :, 0:2048], dwk, SW, writes=["awt"],
            note=f"A1s{s}:wk")
        dma("gpsimd", awt[:, :, 2048:3072], dwv, SW, writes=["awt2"],
            note=f"A1s{s}:wv")
        P.writes["awt"] = (SW, SW.n)
        P.writes["awt2"] = (SW, SW.n)
        awt_ready = [(SW, SW.n)]

        for it in range(NT):
            tok = slice(it * TILE, (it + 1) * TILE)
            gp = gt[0] % 2
            gt[0] += 1
            xa, xar = x_s[gp], f"x_s{gp}"
            xn, xnf = xn_s[gp], "xn" + str(gp) + "_{}"
            tg = f"A1s{s}t{it}"
            dma("sync", xa[:, :, :], dx[s, :, :, tok], SX,
                writes=[xar], note=f"{tg}:x")
            emit_ln(xa, xar, xn, xnf, tg, extra_first=[(SX, SX.n)])
            emit_rope(k12st, "k12st{}", xn, xnf,
                      ctab_s[:, tok], stab_s[:, tok], tg)
            for m in range(4):
                vb = m % 2
                vpair = pair0 if m % 2 == 0 else pair1
                vres = "pair0" if m % 2 == 0 else "pair1"
                for half in range(2):
                    mm_group(
                        [(xn[:, ic, 128 * m:128 * (m + 1)],
                          awt[:, ic, 2048 + 512 * half:2048 + 512 * (half + 1)])
                         for ic in range(CH)],
                        vpair[:, 512 * half:512 * (half + 1)],
                        reads=[xnf.format(ic) for ic in range(CH)] + ["awt2"],
                        writes=[f"{vres}{'ab'[half]}"],
                        extra=awt_ready,
                        note=f"{tg}:v{m}h{half}")
                act(vstage[vb][:, :], vpair[:, :], AF.Copy,
                    reads=[f"{vres}a", f"{vres}b"], writes=[f"vstage{vb}"],
                    note=f"{tg}:vev{m}")
                dma("sync", sp_v[4 * it + m], vstage[vb][:, :], SKV[vb],
                    reads=[f"vstage{vb}"], writes=[f"sp_v{4 * it + m}"],
                    note=f"{tg}:vdma{m}")
            for m in range(4):
                kt_i = 4 * it + m
                for hb in range(2):
                    dma("sync", sp_k[kt_i, hb],
                        k12st[:, 8 * hb:8 * (hb + 1), 128 * m:128 * (m + 1)],
                        SKK,
                        reads=[f"k12st{c}" for c in range(8 * hb, 8 * hb + 8)],
                        writes=[f"sp_k{kt_i}_{hb}"],
                        note=f"{tg}:kdma{m}_{hb}")
            for m in range(4):
                for hb in range(2):
                    P.writes[f"sp_k{4 * it + m}_{hb}"] = (SKK, SKK.n)
            for c in range(16):
                P.guards[f"k12st{c}"] = [(SKK, SKK.n)]
            pace_spill(2)

        dma("gpsimd", awt[:, :, 0:2048], dwq, SW, writes=["awt"],
            note=f"A2s{s}:wq")
        dma("gpsimd", awt[:, :, 2048:3072], dwo, SW, writes=["awt2"],
            note=f"A2s{s}:wo")
        P.writes["awt"] = (SW, SW.n)
        P.writes["awt2"] = (SW, SW.n)
        awt_ready = [(SW, SW.n)]
        kv_done = [(SKK, SKK.n), (SKV[0], SKV[0].n), (SKV[1], SKV[1].n)]

        a2_x = {}

        def a2_prep(j):
            tok = slice(j * TILE, (j + 1) * TILE)
            gp = gt[0] % 2
            gt[0] += 1
            xa, xar = x_s[gp], f"x_s{gp}"
            xn, xnf = xn_s[gp], "xn" + str(gp) + "_{}"
            a2_x[j] = (xa, xar)
            tg = f"A2s{s}q{j}"
            dma("sync", xa[:, :, :], dx[s, :, :, tok], SX,
                writes=[xar], note=f"{tg}:x")
            emit_ln(xa, xar, xn, xnf, tg, extra_first=[(SX, SX.n)])
            emit_rope(q12_s, "q12_{}", xn, xnf,
                      ctab_s[:, tok], stab_s[:, tok], tg)

        a2_prep(0)
        for j in range(NT):
            tok = slice(j * TILE, (j + 1) * TILE)
            nkt = 4 * (j + 1)
            xa, xar = a2_x[j]
            tg = f"A2s{s}q{j}"

            for hb in range(2):
                iters = [(kt_i, hh) for kt_i in range(nkt) for hh in range(8)]
                n_it = len(iters)
                k_rdy = {}
                v_rdy = {}

                def load_slab(kt_i):
                    kb = kt_i % 2
                    dma("sync", kslab[kb][:, :, :], sp_k[kt_i, hb], KSL[kb],
                        reads=[f"sp_k{kt_i}_{hb}"], writes=[f"kslab{kb}"],
                        extra=kv_done,
                        note=f"{tg}b{hb}:kld{kt_i}")
                    k_rdy[kt_i] = [(KSL[kb], KSL[kb].n)]
                    dma("sync", vslab[kb][:, :],
                        sp_v[kt_i][:, 512 * hb:512 * (hb + 1)], VSL[kb],
                        reads=[f"sp_v{kt_i}"], writes=[f"vslab{kb}"],
                        extra=kv_done,
                        note=f"{tg}b{hb}:vld{kt_i}")
                    v_rdy[kt_i] = [(VSL[kb], VSL[kb].n)]

                load_slab(0)
                if nkt > 1:
                    load_slab(1)

                def emit_S(k):
                    kt_i, hh = iters[k]
                    kb = kt_i % 2
                    h = 8 * hb + hh
                    sa, sb = 2 * (k % 2), 2 * (k % 2) + 1
                    mm_group([(kslab[kb][0:64, hh, :], q12_s[0:64, h, :])],
                             pbank[sa][:, :],
                             reads=[f"kslab{kb}", f"q12_{h}"],
                             writes=[f"pb{sa}"], extra=k_rdy[kt_i],
                             note=f"{tg}b{hb}k{kt_i}h{hh}:S1")
                    mm_group([(kslab[kb][64:128, hh, :],
                               q12_s[64:128, h, :])],
                             pbank[sb][:, :],
                             reads=[f"kslab{kb}", f"q12_{h}"],
                             writes=[f"pb{sb}"],
                             note=f"{tg}b{hb}k{kt_i}h{hh}:S2")

                def emit_chain(k):
                    kt_i, hh = iters[k]
                    kb = kt_i % 2
                    sa, sb = 2 * (k % 2), 2 * (k % 2) + 1
                    pbuf = k % 2
                    diag = kt_i >= 4 * j
                    act(s1s[:, :], pbank[sa][:, :], AF.Copy,
                        scale=1.0 / 4096.0,
                        reads=[f"pb{sa}"], writes=["s1s"],
                        note=f"{tg}b{hb}k{kt_i}h{hh}:ev")
                    tt(p_s[:, pbuf, :], pbank[sb][:, :], s1s[:, :],
                       ALU.mult, reads=[f"pb{sb}", "s1s"],
                       writes=[f"p{pbuf}"],
                       note=f"{tg}b{hb}k{kt_i}h{hh}:P")
                    if diag:
                        moff = kt_i - 4 * j

                        def mfn(g, o=p_s[:, pbuf, :], base=-128 * moff):
                            return g.affine_select(
                                out=o, in_=o, pattern=[[1, TILE]],
                                compare_op=ALU.is_ge, fill=0.0,
                                base=base, channel_multiplier=-1)
                        P.ev("gpsimd", mfn, reads=[f"p{pbuf}"],
                             writes=[f"p{pbuf}"],
                             note=f"{tg}b{hb}k{kt_i}h{hh}:mask")
                    pt = pair0 if hh < 4 else pair1
                    colh = (hh % 4) // 2
                    row0 = 64 * (hh % 2)
                    yres = f"pair{hh // 4}{'ab'[colh]}"
                    mm_group(
                        [(vslab[kb][:, 64 * hh:64 * hh + 64],
                          p_s[:, pbuf, :])],
                        pt[row0:row0 + 64, 512 * colh:512 * (colh + 1)],
                        reads=[f"p{pbuf}", f"vslab{kb}"],
                        writes=[yres], extra=v_rdy[kt_i],
                        note=f"{tg}b{hb}k{kt_i}h{hh}:PV",
                        start=(kt_i == 0), stop=(kt_i == nkt - 1),
                        skip=True)

                for k in range(n_it + 2):
                    if k >= 2:
                        emit_chain(k - 2)
                        ktc, hhc = iters[k - 2]
                        if hhc == 7 and ktc + 2 < nkt:
                            load_slab(ktc + 2)
                    if k < n_it:
                        emit_S(k)

                for hh in range(8):
                    h = 8 * hb + hh
                    pt = pair0 if hh < 4 else pair1
                    colh = (hh % 4) // 2
                    row0 = 64 * (hh % 2)
                    yres = f"pair{hh // 4}{'ab'[colh]}"

                    def yfn(sc, o=yfm[row0:row0 + 64, h // 2, :],
                            i=pt[row0:row0 + 64, 512 * colh:512 * (colh + 1)]):
                        return sc.copy(o, i)
                    P.ev("scalar", yfn, reads=[yres],
                         writes=[f"yfm{h // 2}_{row0}"],
                         note=f"{tg}b{hb}:yev{hh}")

            for oc in range(CH):
                pa = oc % 4
                mm_group([(awt[:, dc, 2048 + 128 * oc:2048 + 128 * (oc + 1)],
                           yfm[:, dc, :]) for dc in range(CH)],
                         pbank[pa][:, :],
                         reads=["awt2"] + [f"yfm{dc}_0" for dc in range(CH)]
                               + [f"yfm{dc}_64" for dc in range(CH)],
                         writes=[f"pb{pa}"], extra=awt_ready,
                         note=f"{tg}:wo{oc}")
                act(out_s[:, oc, :], pbank[pa][:, :], AF.Copy,
                    reads=[f"pb{pa}"], writes=["out_s"],
                    note=f"{tg}:woev{oc}")
            if j + 1 < NT:
                a2_prep(j + 1)
            pace_spill(2)
            emit_ln_residual(xa, xar, f"{tg}:ln3")
            dma("sync", sp_x2[s, :, :, tok], out_s[:, :, :], SX2,
                reads=["out_s"], writes=[f"sp_x2_{s}_{j}"],
                note=f"{tg}:x2dma")

    finish_spills()

    bar_idx = len(P.events)
    bar_waits = [(CT, CT.n), (CS, CS.n), (CV, CV.n), (CG, CG.n),
                 (SX2, SX2.n)]

    a_stack.close()
    gten = E(nc.sbuf_tensor("gten", [128, HCH, TILE], BF16))
    uten = E(nc.sbuf_tensor("uten", [128, HCH, TILE], BF16))
    hslab = [E(nc.sbuf_tensor(f"hslab{i}", [128, HCH, 128], BF16))
             for i in range(2)]
    ab0 = E(nc.sbuf_tensor("ab0", [128, TILE], BF16))
    ab1 = E(nc.sbuf_tensor("ab1", [128, TILE], BF16))

    spill_done = [(SPL, SPL.n)]
    NB = SEQ * NT

    def b_parity(n):
        return (gt[0] + n) % 2

    def b_ld_ln(n):
        s2, j2 = n // NT, n % NT
        tok2 = slice(j2 * TILE, (j2 + 1) * TILE)
        gp = b_parity(n)
        xa, xar = x_s[gp], f"x_s{gp}"
        xn, xnf = xn_s[gp], "xn" + str(gp) + "_{}"
        dma("sync", xa[:, :, :], sp_x2[s2, :, :, tok2], SX,
            reads=[f"sp_x2_{s2}_{j2}"], writes=[xar],
            extra=spill_done, note=f"B{n}:x2ld")
        emit_ln(xa, xar, xn, xnf, f"B{n}:ln2", extra_first=[(SX, SX.n)])

    b_ld_ln(0)

    for n in range(NB):
        s2, j2 = n // NT, n % NT
        gp = b_parity(n)
        xa, xar = x_s[gp], f"x_s{gp}"
        xn, xnf = xn_s[gp], "xn" + str(gp) + "_{}"
        tg = f"B{n}"

        for grp in range(8):
            sl1, sl2 = (2 * grp) % 6, (2 * grp + 1) % 6
            dma("gpsimd", awt[:, :, 512 * sl1:512 * (sl1 + 1)], sp_w1[grp],
                WSL[sl1], reads=["sp_w1"], writes=[f"wsl{sl1}"],
                note=f"{tg}:w1ld{grp}")
            dma("gpsimd", awt[:, :, 512 * sl2:512 * (sl2 + 1)], sp_w2[grp],
                WSL[sl2], reads=["sp_w2"], writes=[f"wsl{sl2}"],
                note=f"{tg}:w2ld{grp}")
            r1 = [(WSL[sl1], WSL[sl1].n)]
            r2 = [(WSL[sl2], WSL[sl2].n)]
            for q in range(4):
                oc = 4 * grp + q
                pa, pb2 = 2 * (oc % 2), 2 * (oc % 2) + 1
                mm_group([(awt[:, ic,
                               512 * sl1 + 128 * q:512 * sl1 + 128 * (q + 1)],
                           xn[:, ic, :]) for ic in range(CH)],
                         pbank[pa][:, :],
                         reads=[f"wsl{sl1}"] + [xnf.format(ic)
                                                for ic in range(CH)],
                         writes=[f"pb{pa}"], extra=r1,
                         note=f"{tg}:w1mm{oc}")
                mm_group([(awt[:, ic,
                               512 * sl2 + 128 * q:512 * sl2 + 128 * (q + 1)],
                           xn[:, ic, :]) for ic in range(CH)],
                         pbank[pb2][:, :],
                         reads=[f"wsl{sl2}"] + [xnf.format(ic)
                                                for ic in range(CH)],
                         writes=[f"pb{pb2}"], extra=r2,
                         note=f"{tg}:w2mm{oc}")
                act(ab0[:, :], pbank[pa][:, :], AF.Silu,
                    scale=scl_s["s1v"][:, oc:oc + 1],
                    reads=[f"pb{pa}", "s1v"], writes=["ab0"],
                    note=f"{tg}:silu{oc}")
                act(ab1[:, :], pbank[pb2][:, :], AF.Copy,
                    scale=scl_s["s2v"][:, oc:oc + 1],
                    reads=[f"pb{pb2}", "s2v"], writes=["ab1"],
                    note=f"{tg}:w2ev{oc}")
                tt(gten[:, oc, :], ab0[:, :], ab1[:, :], ALU.mult,
                   reads=["ab0", "ab1"], writes=[f"g{oc}"],
                   note=f"{tg}:g{oc}")

        for oc in range(HCH):
            hb2 = oc % 2
            dma("gpsimd", hslab[hb2][:, :, :], sp_h1[oc], HSL[hb2],
                reads=["sp_h1"], writes=[f"hslab{hb2}"], note=f"{tg}:h1ld{oc}")
            hr = [(HSL[hb2], HSL[hb2].n)]
            pa = oc % 4
            mm_group([(hslab[hb2][:, ic, :], gten[:, ic, :])
                      for ic in range(HCH)],
                     pbank[pa][:, :],
                     reads=[f"hslab{hb2}"] + [f"g{ic}" for ic in range(HCH)],
                     writes=[f"pb{pa}"], extra=hr, note=f"{tg}:h1mm{oc}")
            act(uten[:, oc, :], pbank[pa][:, :], AF.Gelu,
                scale=scl_s["s3v"][:, oc:oc + 1],
                reads=[f"pb{pa}", "s3v"], writes=[f"u{oc}"],
                note=f"{tg}:gelu1_{oc}")

        if n + 1 < NB:
            b_ld_ln(n + 1)

        for oc in range(HCH):
            hb2 = oc % 2
            dma("gpsimd", hslab[hb2][:, :, :], sp_h2[oc], HSL[hb2],
                reads=["sp_h2"], writes=[f"hslab{hb2}"], note=f"{tg}:h2ld{oc}")
            hr = [(HSL[hb2], HSL[hb2].n)]
            pa = oc % 4
            mm_group([(hslab[hb2][:, ic, :], uten[:, ic, :])
                      for ic in range(HCH)],
                     pbank[pa][:, :],
                     reads=[f"hslab{hb2}"] + [f"u{ic}" for ic in range(HCH)],
                     writes=[f"pb{pa}"], extra=hr, note=f"{tg}:h2mm{oc}")
            act(gten[:, oc, :], pbank[pa][:, :], AF.Gelu,
                scale=scl_s["s4v"][:, oc:oc + 1],
                reads=[f"pb{pa}", "s4v"], writes=[f"g{oc}"],
                note=f"{tg}:gelu2_{oc}")

        for oc in range(CH):
            hb2 = oc % 2
            dma("gpsimd", hslab[hb2][:, :, :], sp_w3[oc], HSL[hb2],
                reads=["sp_w3"], writes=[f"hslab{hb2}"], note=f"{tg}:w3ld{oc}")
            hr = [(HSL[hb2], HSL[hb2].n)]
            pa = oc % 4
            mm_group([(hslab[hb2][:, ic, :], gten[:, ic, :])
                      for ic in range(HCH)],
                     pbank[pa][:, :],
                     reads=[f"hslab{hb2}"] + [f"g{ic}" for ic in range(HCH)],
                     writes=[f"pb{pa}"], extra=hr, note=f"{tg}:w3mm{oc}")
            act(out_s[:, oc, :], pbank[pa][:, :], AF.Copy,
                scale=scl_s["s5v"][:, oc:oc + 1],
                reads=[f"pb{pa}", "s5v"], writes=["out_s"],
                note=f"{tg}:w3ev{oc}")

        emit_ln_residual(xa, xar, f"{tg}:ln4")
        dma("sync", dy[:, :, s2 * T + j2 * TILE:s2 * T + (j2 + 1) * TILE],
            out_s[:, :, :], SO, reads=["out_s"], writes=[],
            note=f"{tg}:ydma")

    seen_eng = set()
    for i in range(bar_idx, len(P.events)):
        e = P.events[i]
        if e["eng"] in seen_eng:
            continue
        seen_eng.add(e["eng"])
        own = P.eng_ctr.get(e["eng"])
        merged = {sm: v for sm, v in e["waits"]}
        for sm, v in bar_waits:
            if sm is own or v <= 0:
                continue
            if merged.get(sm, 0) < v:
                merged[sm] = v
        e["waits"] = list(merged.items())

    P.check()

    with nc.Block() as block:
        def run_engine(eng_obj, name):
            seen = {}
            for e2 in P.events:
                if e2["eng"] != name:
                    continue
                for s2_, v in e2["waits"]:
                    if seen.get(s2_, 0) >= v:
                        continue
                    seen[s2_] = v
                    eng_obj.wait_ge(s2_.h, v)
                ins = e2["fn"](eng_obj)
                s2_, _v = e2["inc"]
                ins.then_inc(s2_.h, 16 if s2_.dma else 1)
            if name == "sync":
                eng_obj.wait_ge(SO.h, 16 * SEQ * NT)

        @block.tensor
        def _(t):
            run_engine(t, "tensor")

        @block.scalar
        def _(sc):
            run_engine(sc, "scalar")

        @block.vector
        def _(v):
            run_engine(v, "vector")

        @block.gpsimd
        def _(g):
            run_engine(g, "gpsimd")

        @block.sync
        def _(sy):
            run_engine(sy, "sync")

    stack.close()
    return nc



def kernel(**inputs):
    if "nc" not in _CACHED:
        _CACHED["nc"] = _build()
    nc = _CACHED["nc"]

    w = _pack_weights(inputs)
    consts = _host_consts()
    xs = _pack_x(np.asarray(inputs["L0_attn_x"], np.float32))

    base = {}
    base.update(w)
    base.update(consts)
    in_maps = []
    for cid in range(NCORE):
        m = dict(base)
        m["x"] = xs[cid]
        in_maps.append(m)

    res = run_bass_kernel_spmd(nc, in_maps, core_ids=list(range(NCORE)))
    return _unpack_y([r["y"] for r in res.results])


# version 07
# speedup vs baseline: 1.1023x; 1.1023x over previous
import numpy as np
import ml_dtypes
import concourse.bass as bass
import concourse.mybir as mybir
from concourse.bass_utils import run_bass_kernel_spmd

F32 = mybir.dt.float32
BF16 = mybir.dt.bfloat16
AF = mybir.ActivationFunctionType
ALU = mybir.AluOpType
bfdt = ml_dtypes.bfloat16

B, T, C = 16, 2048, 1024
H, D = 16, 64
HID = 4096
NCORE = 8
SEQ = 2
TILE = 512
NT = T // TILE
CH = C // 128
HCH = HID // 128
KT = T // 128
EPS = 1e-5

_CACHED = {}



def _pack_weights(inp):
    w = {}

    def cat(p1, p2):
        a = np.asarray(p1).reshape(H, 64, C)
        b = np.asarray(p2).reshape(H, 64, C)
        return np.stack([a, b], axis=1).reshape(2 * C, C)

    def to_sb(mat, och):
        t = np.asarray(mat).T.reshape(CH, 128, och * 128).transpose(1, 0, 2)
        return np.ascontiguousarray(t).astype(np.float32)

    w["wkcat"] = to_sb(cat(inp["L0_attn_Wk1"], inp["L0_attn_Wk2"]), 16)
    w["wqcat"] = to_sb(cat(inp["L0_attn_Wq1"], inp["L0_attn_Wq2"]), 16)
    w["wv"] = to_sb(inp["L0_attn_Wv"], 8)
    w["wout"] = to_sb(inp["L0_attn_Wout"], 8)

    def ffn_chunks(mat):
        mat = np.asarray(mat)
        o, i = mat.shape
        noc, nic = o // 128, i // 128
        t = mat.T.reshape(nic, 128, noc, 128).transpose(2, 1, 0, 3)
        return np.ascontiguousarray(t).astype(np.float32)

    def ffn_groups(mat):
        mat = np.asarray(mat)
        t = mat.T.reshape(CH, 128, 8, 512).transpose(2, 1, 0, 3)
        return np.ascontiguousarray(t).astype(np.float32)

    w["w1src"] = ffn_groups(inp["L0_ffn_w1_w"])
    w["w2src"] = ffn_groups(inp["L0_ffn_w2_w"])
    w["h1src"] = ffn_chunks(inp["L0_ffn_h1_w"])
    w["h2src"] = ffn_chunks(inp["L0_ffn_h2_w"])
    w["w3src"] = ffn_chunks(inp["L0_ffn_w3_w"])

    def scl(v):
        return np.ascontiguousarray(np.asarray(v).reshape(-1, 128).T).astype(
            np.float32)

    w["s1v"] = scl(inp["L0_ffn_w1_s"])
    w["s2v"] = scl(inp["L0_ffn_w2_s"])
    w["s3v"] = scl(inp["L0_ffn_h1_s"])
    w["s4v"] = scl(inp["L0_ffn_h2_s"])
    w["s5v"] = scl(inp["L0_ffn_w3_s"])
    return w


def _host_consts():
    c = {}
    inv = 1.0 / (10000.0 ** (np.arange(0, D, 2, dtype=np.float32) / D))
    freqs = np.arange(T, dtype=np.float32)[:, None] * inv[None, :]
    cos = np.cos(freqs).astype(bfdt).astype(np.float32).T
    sin = np.sin(freqs).astype(bfdt).astype(np.float32).T
    c["ctab"] = np.ascontiguousarray(np.tile(cos, (4, 1)))
    c["stab"] = np.ascontiguousarray(
        np.concatenate([-sin, sin, -sin, sin], axis=0))
    m = np.arange(128)
    swp = np.where((m % 64) < 32, m + 32, m - 32)
    S = np.zeros((128, 128), np.float32)
    S[swp, m] = 1.0
    c["swapm"] = S
    c["onesm"] = np.full((128, 128), 1.0 / C, np.float32)
    return c


def _pack_x(x):
    out = []
    for cid in range(NCORE):
        xc = x[2 * cid:2 * cid + 2]
        t = xc.transpose(0, 2, 1).reshape(SEQ, CH, 128, T).transpose(0, 2, 1, 3)
        out.append(np.ascontiguousarray(t).astype(np.float32))
    return out


def _unpack_y(parts):
    outs = []
    for arr in parts:
        a = arr.reshape(128, CH, SEQ, T).transpose(2, 3, 1, 0)
        outs.append(a.reshape(SEQ, T, C))
    return np.ascontiguousarray(np.concatenate(outs, axis=0))



class _Sem:
    __slots__ = ("h", "n", "name", "dma")

    def __init__(self, h, name, dma=False):
        self.h = h
        self.n = 0
        self.name = name
        self.dma = dma


class _Prog:
    def __init__(self):
        self.events = []
        self.writes = {}
        self.guards = {}
        self.eng_ctr = {}

    def ev(self, eng, fn, reads=(), writes=(), extra=(), dma_sem=None, note=""):
        waits = {}

        def add(pair):
            if pair is None:
                return
            s, v = pair
            if v <= 0:
                return
            if waits.get(s, 0) < v:
                waits[s] = v

        for r in reads:
            add(self.writes.get(r))
        for r in writes:
            for p in self.guards.get(r, ()):
                add(p)
            add(self.writes.get(r))
        for p in extra:
            add(p)

        if dma_sem is not None:
            dma_sem.n += 16
            inc = (dma_sem, dma_sem.n)
        else:
            ctr = self.eng_ctr[eng]
            ctr.n += 1
            inc = (ctr, ctr.n)

        own = self.eng_ctr.get(eng)
        wl = [(s, v) for s, v in waits.items() if s is not own]

        self.events.append(dict(eng=eng, fn=fn, waits=wl, inc=inc, note=note))
        for r in reads:
            self.guards.setdefault(r, []).append(inc)
        for r in writes:
            self.writes[r] = inc
            self.guards[r] = []
        return inc

    def check(self):
        eng_names = sorted({e["eng"] for e in self.events})
        queues = {n: [e for e in self.events if e["eng"] == n]
                  for n in eng_names}
        ptr = {n: 0 for n in eng_names}
        val = {}
        total = len(self.events)
        done = 0
        progress = True
        while progress:
            progress = False
            for n in eng_names:
                while ptr[n] < len(queues[n]):
                    e = queues[n][ptr[n]]
                    if all(val.get(s, 0) >= v for s, v in e["waits"]):
                        s, v = e["inc"]
                        val[s] = max(val.get(s, 0), v)
                        ptr[n] += 1
                        done += 1
                        progress = True
                    else:
                        break
        if done != total:
            lines = []
            for n in eng_names:
                if ptr[n] < len(queues[n]):
                    e = queues[n][ptr[n]]
                    missing = [(s.name, v, val.get(s, 0))
                               for s, v in e["waits"] if val.get(s, 0) < v]
                    lines.append(f"  {n}@{ptr[n]}/{len(queues[n])} "
                                 f"'{e['note']}' missing {missing}")
            raise RuntimeError("schedule deadlock:\n" + "\n".join(lines))



def _build(dbg=False):
    from contextlib import ExitStack

    nc = bass.Bass("TRN2", target_bir_lowering=False, debug=False)
    P = _Prog()

    dx = nc.dram_tensor("x", [SEQ, 128, CH, T], F32, kind="ExternalInput").ap()
    dwk = nc.dram_tensor("wkcat", [128, CH, 2048], F32, kind="ExternalInput").ap()
    dwq = nc.dram_tensor("wqcat", [128, CH, 2048], F32, kind="ExternalInput").ap()
    dwv = nc.dram_tensor("wv", [128, CH, 1024], F32, kind="ExternalInput").ap()
    dwo = nc.dram_tensor("wout", [128, CH, 1024], F32, kind="ExternalInput").ap()
    dct = nc.dram_tensor("ctab", [128, T], F32, kind="ExternalInput").ap()
    dstb = nc.dram_tensor("stab", [128, T], F32, kind="ExternalInput").ap()
    dsw = nc.dram_tensor("swapm", [128, 128], F32, kind="ExternalInput").ap()
    don = nc.dram_tensor("onesm", [128, 128], F32, kind="ExternalInput").ap()
    dw1 = nc.dram_tensor("w1src", [8, 128, CH, 512], F32, kind="ExternalInput").ap()
    dw2 = nc.dram_tensor("w2src", [8, 128, CH, 512], F32, kind="ExternalInput").ap()
    dh1 = nc.dram_tensor("h1src", [HCH, 128, HCH, 128], F32, kind="ExternalInput").ap()
    dh2 = nc.dram_tensor("h2src", [HCH, 128, HCH, 128], F32, kind="ExternalInput").ap()
    dw3 = nc.dram_tensor("w3src", [CH, 128, HCH, 128], F32, kind="ExternalInput").ap()
    dscl = {}
    for nme, cols in (("s1v", HCH), ("s2v", HCH), ("s3v", HCH), ("s4v", HCH),
                      ("s5v", CH)):
        dscl[nme] = nc.dram_tensor(nme, [128, cols], F32,
                                   kind="ExternalInput").ap()
    dy = nc.dram_tensor("y", [128, CH, SEQ * T], F32, kind="ExternalOutput").ap()

    sp_wk = nc.dram_tensor("sp_wk", [128, CH, 2048], BF16, kind="Internal").ap()
    sp_wq = nc.dram_tensor("sp_wq", [128, CH, 2048], BF16, kind="Internal").ap()
    sp_wv = nc.dram_tensor("sp_wv", [128, CH, 1024], BF16, kind="Internal").ap()
    sp_wo = nc.dram_tensor("sp_wo", [128, CH, 1024], BF16, kind="Internal").ap()
    sp_w1 = nc.dram_tensor("sp_w1", [8, 128, CH, 512], BF16, kind="Internal").ap()
    sp_w2 = nc.dram_tensor("sp_w2", [8, 128, CH, 512], BF16, kind="Internal").ap()
    sp_h1 = nc.dram_tensor("sp_h1", [HCH, 128, HCH, 128], BF16, kind="Internal").ap()
    sp_h2 = nc.dram_tensor("sp_h2", [HCH, 128, HCH, 128], BF16, kind="Internal").ap()
    sp_w3 = nc.dram_tensor("sp_w3", [CH, 128, HCH, 128], BF16, kind="Internal").ap()
    kdbg = "ExternalOutput" if dbg else "Internal"
    sp_k = nc.dram_tensor("sp_k", [NT, 128, 4, 16, 128], BF16, kind=kdbg).ap()
    sp_v = nc.dram_tensor("sp_v", [KT, 128, 1024], BF16, kind=kdbg).ap()
    sp_x2 = nc.dram_tensor("sp_x2", [SEQ, 128, CH, T], F32,
                           kind="ExternalOutput" if dbg else "Internal").ap()

    stack = ExitStack()
    E = stack.enter_context

    def sem(name, dma=False):
        return _Sem(E(nc.semaphore(name)), name, dma)

    SW = sem("SW", True)
    SC = sem("SC", True)
    SPL = sem("SPL", True)
    SX = sem("SX", True)
    SKK = sem("SKK", True)
    SKV = [sem("SKV0", True), sem("SKV1", True)]
    KVS = [sem(f"KVS{i}", True) for i in range(4)]
    HSL = [sem("HSL0", True), sem("HSL1", True)]
    WSL = [sem(f"WSL{i}", True) for i in range(6)]
    SX2 = sem("SX2", True)
    SO = sem("SO", True)
    CT = sem("CT")
    CS = sem("CS")
    CV = sem("CV")
    CG = sem("CG")
    P.eng_ctr = {"tensor": CT, "scalar": CS, "vector": CV, "gpsimd": CG}

    awt = E(nc.sbuf_tensor("awt", [128, CH, 3072], BF16))
    x_s = [E(nc.sbuf_tensor(f"x_s{i}", [128, CH, TILE], F32)) for i in range(2)]
    xn_s = [E(nc.sbuf_tensor(f"xn_s{i}", [128, CH, TILE], BF16))
            for i in range(2)]
    cbf = [E(nc.sbuf_tensor(f"cbf{i}", [128, TILE], BF16)) for i in range(2)]
    csq = [E(nc.sbuf_tensor(f"csq{i}", [128, TILE], BF16)) for i in range(2)]
    mean_s = E(nc.sbuf_tensor("mean_s", [128, TILE], F32))
    rstd_s = E(nc.sbuf_tensor("rstd_s", [128, TILE], F32))
    vtmp = E(nc.sbuf_tensor("vtmp", [128, TILE], F32))
    t1_s = E(nc.sbuf_tensor("t1_s", [128, TILE], F32))
    out_s = E(nc.sbuf_tensor("out_s", [128, CH, TILE], F32))
    swap_s = E(nc.sbuf_tensor("swap_s", [128, 128], BF16))
    ones_s = E(nc.sbuf_tensor("ones_s", [128, 128], BF16))
    epsb = E(nc.sbuf_tensor("epsb", [128, 1], F32))
    scl_s = {}
    for nme, cols in (("s1v", HCH), ("s2v", HCH), ("s3v", HCH), ("s4v", HCH),
                      ("s5v", CH)):
        scl_s[nme] = E(nc.sbuf_tensor("sb_" + nme, [128, cols], F32))

    a_stack = ExitStack()
    A_ = a_stack.enter_context
    t2_s = A_(nc.sbuf_tensor("t2_s", [128, TILE], BF16))
    ctab_s = A_(nc.sbuf_tensor("ctab_s", [128, T], F32))
    stab_s = A_(nc.sbuf_tensor("stab_s", [128, T], F32))
    k12st = A_(nc.sbuf_tensor("k12st", [128, 4, 16, 128], BF16))
    vstage = [A_(nc.sbuf_tensor(f"vstage{i}", [128, 1024], BF16))
              for i in range(2)]
    q12_s = A_(nc.sbuf_tensor("q12_s", [128, 16, TILE], BF16))
    kslab = [A_(nc.sbuf_tensor(f"kslab{i}", [128, 8, 128], BF16))
             for i in range(4)]
    vslab = [A_(nc.sbuf_tensor(f"vslab{i}", [128, TILE], BF16))
             for i in range(4)]
    s1s = A_(nc.sbuf_tensor("s1s", [128, TILE], F32))
    p_s = A_(nc.sbuf_tensor("p_s", [128, 2, TILE], BF16))
    yfm = A_(nc.sbuf_tensor("yfm", [128, CH, TILE], BF16))

    pair0 = E(nc.psum_tensor("pair0", [128, 1024], F32))
    pair1 = E(nc.psum_tensor("pair1", [128, 1024], F32))
    pbank = [E(nc.psum_tensor(f"pb{i}", [128, 512], F32)) for i in range(4)]

    def dma(eng, out_ap, in_ap, s, reads=(), writes=(), extra=(), note=""):
        def fn(e, o=out_ap, i=in_ap):
            return e.dma_start(out=o, in_=i)
        return P.ev(eng, fn, reads=reads, writes=writes, extra=extra,
                    dma_sem=s, note=note)

    def mm_group(pairs, out_ap, reads, writes, extra=(), note="", start=True,
                 stop=True, skip=False):
        def fn(t, pr=tuple(pairs), o=out_ap, st=start, sp=stop, sk=skip):
            ins = None
            n = len(pr)
            for i, (l, r) in enumerate(pr):
                ins = t.matmul(o, l, r, start=(st and i == 0),
                               stop=(sp and i == n - 1), skip_group_check=sk)
            return ins
        return P.ev("tensor", fn, reads=reads, writes=writes, extra=extra,
                    note=note)

    def act(out_ap, in_ap, func, reads, writes, scale=1.0, bias=0.0, extra=(),
            note=""):
        def fn(s, o=out_ap, i=in_ap, f=func, sc=scale, b=bias):
            return s.activation(o, i, f, scale=sc, bias=b)
        return P.ev("scalar", fn, reads=reads, writes=writes, extra=extra,
                    note=note)

    def tt(out_ap, a_ap, b_ap, op, reads, writes, extra=(), note=""):
        def fn(v, o=out_ap, a=a_ap, b=b_ap, p=op):
            return v.tensor_tensor(o, a, b, p)
        return P.ev("vector", fn, reads=reads, writes=writes, extra=extra,
                    note=note)

    def vcopy(out_ap, in_ap, reads, writes, extra=(), note=""):
        def fn(v, o=out_ap, i=in_ap):
            return v.tensor_copy(o, i)
        return P.ev("vector", fn, reads=reads, writes=writes, extra=extra,
                    note=note)

    def vrecip(out_ap, in_ap, reads, writes, note=""):
        def fn(v, o=out_ap, i=in_ap):
            return v.reciprocal(o, i)
        return P.ev("vector", fn, reads=reads, writes=writes, note=note)

    def r3(ap):
        return ap.rearrange("p (m t) -> p m t", m=4)

    def emit_ln_stats(src, src_res, tag, extra_first=()):
        for c in range(CH):
            pp = c % 2
            vcopy(cbf[pp][:, :], src[:, c, :],
                  reads=[src_res], writes=[f"cbf{pp}"],
                  extra=extra_first if c == 0 else (),
                  note=f"{tag}:cast{c}")
            tt(csq[pp][:, :], cbf[pp][:, :], cbf[pp][:, :], ALU.mult,
               reads=[f"cbf{pp}"], writes=[f"csq{pp}"], note=f"{tag}:sq{c}")
            mm_group([(ones_s[:, :], cbf[pp][:, :])], pbank[0][:, :],
                     reads=[f"cbf{pp}", "ones_s"],
                     writes=["pb0"] if c == 0 else [],
                     extra=() if c == 0 else [P.writes.get("pb0")],
                     note=f"{tag}:mmm{c}", start=(c == 0), stop=(c == CH - 1))
            if c != 0:
                P.writes["pb0"] = P.events[-1]["inc"]
            mm_group([(ones_s[:, :], csq[pp][:, :])], pbank[1][:, :],
                     reads=[f"csq{pp}", "ones_s"],
                     writes=["pb1"] if c == 0 else [],
                     extra=() if c == 0 else [P.writes.get("pb1")],
                     note=f"{tag}:mms{c}", start=(c == 0), stop=(c == CH - 1))
            if c != 0:
                P.writes["pb1"] = P.events[-1]["inc"]
        act(mean_s[:, :], pbank[0][:, :], AF.Copy, reads=["pb0"],
            writes=["mean_s"], note=f"{tag}:mean_ev")
        tt(vtmp[:, :], mean_s[:, :], mean_s[:, :], ALU.mult,
           reads=["mean_s"], writes=["vtmp"], note=f"{tag}:mean2")
        tt(vtmp[:, :], pbank[1][:, :], vtmp[:, :], ALU.subtract,
           reads=["pb1"], writes=["vtmp"], note=f"{tag}:var")
        act(t1_s[:, :], vtmp[:, :], AF.Sqrt, scale=1.0, bias=epsb[:, 0:1],
            reads=["vtmp", "epsb"], writes=["t1_s"], note=f"{tag}:sqrt")
        vrecip(rstd_s[:, :], t1_s[:, :], reads=["t1_s"], writes=["rstd_s"],
               note=f"{tag}:recip")

    def emit_ln(src, src_res, xn, xn_fmt, tag, extra_first=()):
        emit_ln_stats(src, src_res, tag, extra_first)
        for c in range(CH):
            tt(t1_s[:, :], src[:, c, :], mean_s[:, :], ALU.subtract,
               reads=[src_res, "mean_s"], writes=["t1_s"], note=f"{tag}:sub{c}")
            tt(xn[:, c, :], t1_s[:, :], rstd_s[:, :], ALU.mult,
               reads=["t1_s", "rstd_s"], writes=[xn_fmt.format(c)],
               note=f"{tag}:nrm{c}")

    def emit_ln_residual(xr, xr_res, tag):
        emit_ln_stats(out_s, "out_s", tag)
        for c in range(CH):
            tt(t1_s[:, :], out_s[:, c, :], mean_s[:, :], ALU.subtract,
               reads=["out_s", "mean_s"], writes=["t1_s"], note=f"{tag}:sub{c}")
            tt(vtmp[:, :], t1_s[:, :], rstd_s[:, :], ALU.mult,
               reads=["t1_s", "rstd_s"], writes=["vtmp"], note=f"{tag}:nrm{c}")
            tt(out_s[:, c, :], vtmp[:, :], xr[:, c, :], ALU.add,
               reads=["vtmp", xr_res], writes=["out_s"], note=f"{tag}:res{c}")

    def emit_rope(dst_fn, wrap3, xn, xn_fmt, ct_sl, st_sl, tag):
        def proj(c):
            pa = c % 2
            mm_group([(awt[:, ic, 128 * c:128 * (c + 1)], xn[:, ic, :])
                      for ic in range(CH)],
                     pbank[pa][:, :],
                     reads=[f"awtA{c // 4}"] + [xn_fmt.format(ic)
                                               for ic in range(CH)],
                     writes=[f"pb{pa}"], note=f"{tag}:proj{c}")

        def tail(c):
            pa, pb = c % 2, 2 + c % 2
            tt(t1_s[:, :], pbank[pa][:, :], ct_sl, ALU.mult,
               reads=[f"pb{pa}", "ctab"], writes=["t1_s"], note=f"{tag}:ct{c}")
            tt(t2_s[:, :], pbank[pa][:, :], st_sl, ALU.mult,
               reads=[f"pb{pa}", "stab"], writes=["t2_s"], note=f"{tag}:st{c}")
            mm_group([(swap_s[:, :], t2_s[:, :])], pbank[pb][:, :],
                     reads=["t2_s", "swap_s"], writes=[f"pb{pb}"],
                     note=f"{tag}:swap{c}")
            dap, dres = dst_fn(c)
            if wrap3:
                tt(dap, r3(pbank[pb][:, :]), r3(t1_s[:, :]), ALU.add,
                   reads=[f"pb{pb}", "t1_s"], writes=[dres],
                   note=f"{tag}:rot{c}")
            else:
                tt(dap, pbank[pb][:, :], t1_s[:, :], ALU.add,
                   reads=[f"pb{pb}", "t1_s"], writes=[dres],
                   note=f"{tag}:rot{c}")

        for c in range(17):
            if c < 16:
                proj(c)
            if c >= 1:
                tail(c - 1)

    def msfn(v, o=epsb[:, 0:1], cc=EPS):
        return v.memset(o, cc)
    P.ev("vector", msfn, writes=["epsb"], note="memset eps")

    dma("gpsimd", ones_s[:, :], don, SC, writes=["ones_s"], note="ld ones")
    dma("gpsimd", swap_s[:, :], dsw, SC, writes=["swap_s"], note="ld swap")
    dma("scalar", ctab_s[:, :], dct, SC, writes=["ctab"], note="ld ctab")
    dma("scalar", stab_s[:, :], dstb, SC, writes=["stab"], note="ld stab")
    for nme in scl_s:
        dma("scalar", scl_s[nme][:, :], dscl[nme], SC, writes=[nme],
            note=f"ld {nme}")

    for q in range(4):
        dma("gpsimd", awt[:, :, 512 * q:512 * (q + 1)],
            dwk[:, :, 512 * q:512 * (q + 1)], SW,
            writes=[f"awtA{q}"], note=f"wk0:A{q}")
    for hh in range(2):
        dma("gpsimd", awt[:, :, 2048 + 512 * hh:2048 + 512 * (hh + 1)],
            dwv[:, :, 512 * hh:512 * (hh + 1)], SW,
            writes=[f"awtB{hh}"], note=f"wv0:B{hh}")
    dma("gpsimd", sp_wq, dwq, SPL, writes=["sp_wqd"], note="spill wq")
    dma("gpsimd", sp_wo, dwo, SPL, writes=["sp_wod"], note="spill wo")
    dma("gpsimd", sp_wk, dwk, SPL, writes=["sp_wkd"], note="spill wk")
    dma("gpsimd", sp_wv, dwv, SPL, writes=["sp_wvd"], note="spill wv")

    spill_pieces = []
    for qq in range(2):
        spill_pieces.append((sp_w1[4 * qq:4 * (qq + 1)],
                             dw1[4 * qq:4 * (qq + 1)], f"sp_w1p{qq}"))
        spill_pieces.append((sp_w2[4 * qq:4 * (qq + 1)],
                             dw2[4 * qq:4 * (qq + 1)], f"sp_w2p{qq}"))
        spill_pieces.append((sp_w3[4 * qq:4 * (qq + 1)],
                             dw3[4 * qq:4 * (qq + 1)], f"sp_w3p{qq}"))
    for qq in range(8):
        spill_pieces.append((sp_h1[4 * qq:4 * (qq + 1)],
                             dh1[4 * qq:4 * (qq + 1)], f"sp_h1p{qq}"))
    for qq in range(8):
        spill_pieces.append((sp_h2[4 * qq:4 * (qq + 1)],
                             dh2[4 * qq:4 * (qq + 1)], f"sp_h2p{qq}"))
    spill_i = [0]

    def pace_spill(k=2):
        for _ in range(k):
            if spill_i[0] >= len(spill_pieces):
                return
            d2, s2_, res = spill_pieces[spill_i[0]]
            spill_i[0] += 1
            dma("gpsimd", d2, s2_, SPL, writes=[res], extra=[(CT, CT.n)],
                note=f"spill {res}")

    def finish_spills():
        while spill_i[0] < len(spill_pieces):
            pace_spill(4)

    def load_awtA(src, res, tag):
        for q in range(4):
            dma("scalar", awt[:, :, 512 * q:512 * (q + 1)],
                src[:, :, 512 * q:512 * (q + 1)], SW,
                reads=[res], writes=[f"awtA{q}"], note=f"{tag}:A{q}")

    def load_awtB(src, res, tag):
        for hh in range(2):
            dma("scalar", awt[:, :, 2048 + 512 * hh:2048 + 512 * (hh + 1)],
                src[:, :, 512 * hh:512 * (hh + 1)], SW,
                reads=[res], writes=[f"awtB{hh}"], note=f"{tag}:B{hh}")

    gt = [0]

    for s in range(SEQ):
        if s > 0:
            load_awtA(sp_wk, "sp_wkd", f"A1s{s}wk")
            load_awtB(sp_wv, "sp_wvd", f"A1s{s}wv")

        for it in range(NT):
            tok = slice(it * TILE, (it + 1) * TILE)
            gp = gt[0] % 2
            gt[0] += 1
            xa, xar = x_s[gp], f"x_s{gp}"
            xn, xnf = xn_s[gp], "xn" + str(gp) + "_{}"
            tg = f"A1s{s}t{it}"
            dma("sync", xa[:, :, :], dx[s, :, :, tok], SX,
                writes=[xar], note=f"{tg}:x")
            emit_ln(xa, xar, xn, xnf, tg, extra_first=[(SX, SX.n)])
            emit_rope(lambda c: (k12st[:, :, c, :], f"k12st{c}"), True,
                      xn, xnf, ctab_s[:, tok], stab_s[:, tok], tg)
            for m in range(4):
                vb = m % 2
                vpair = pair0 if m % 2 == 0 else pair1
                vres = "pair0" if m % 2 == 0 else "pair1"
                for half in range(2):
                    mm_group(
                        [(xn[:, ic, 128 * m:128 * (m + 1)],
                          awt[:, ic, 2048 + 512 * half:2048 + 512 * (half + 1)])
                         for ic in range(CH)],
                        vpair[:, 512 * half:512 * (half + 1)],
                        reads=[xnf.format(ic) for ic in range(CH)]
                              + [f"awtB{half}"],
                        writes=[f"{vres}{'ab'[half]}"],
                        note=f"{tg}:v{m}h{half}")
                act(vstage[vb][:, :], vpair[:, :], AF.Copy,
                    reads=[f"{vres}a", f"{vres}b"], writes=[f"vstage{vb}"],
                    note=f"{tg}:vev{m}")
                dma("sync", sp_v[4 * it + m], vstage[vb][:, :], SKV[vb],
                    reads=[f"vstage{vb}"], writes=[f"sp_v{4 * it + m}"],
                    note=f"{tg}:vdma{m}")
            dma("sync", sp_k[it], k12st[:, :, :, :], SKK,
                reads=[f"k12st{c}" for c in range(16)],
                writes=[f"sp_kt{it}"], note=f"{tg}:kdump")
            pace_spill(2)

        load_awtA(sp_wq, "sp_wqd", f"A2s{s}wq")
        load_awtB(sp_wo, "sp_wod", f"A2s{s}wo")

        a2_x = {}

        def a2_prep(j):
            tok = slice(j * TILE, (j + 1) * TILE)
            gp = gt[0] % 2
            gt[0] += 1
            xa, xar = x_s[gp], f"x_s{gp}"
            xn, xnf = xn_s[gp], "xn" + str(gp) + "_{}"
            a2_x[j] = (xa, xar)
            tg = f"A2s{s}q{j}"
            dma("sync", xa[:, :, :], dx[s, :, :, tok], SX,
                writes=[xar], note=f"{tg}:x")
            emit_ln(xa, xar, xn, xnf, tg, extra_first=[(SX, SX.n)])
            emit_rope(lambda c: (q12_s[:, c, :], f"q12_{c}"), False,
                      xn, xnf, ctab_s[:, tok], stab_s[:, tok], tg)

        a2_prep(0)

        cons = [(j2, hb2, kt2) for j2 in range(NT) for hb2 in range(2)
                for kt2 in range(4 * (j2 + 1))]
        k_rdy = {}
        next_load = [0]

        def emit_slab_load(ci):
            j2, hb2, kt2 = cons[ci]
            bu = ci % 4
            it2, m2 = divmod(kt2, 4)
            dma("sync", kslab[bu][:, :, :],
                sp_k[it2, :, m2, 8 * hb2:8 * (hb2 + 1), :], KVS[bu],
                reads=[f"sp_kt{it2}"], writes=[f"kslab{bu}"],
                note=f"s{s}ci{ci}:kld")
            dma("sync", vslab[bu][:, :],
                sp_v[kt2][:, 512 * hb2:512 * (hb2 + 1)], KVS[bu],
                reads=[f"sp_v{kt2}"], writes=[f"vslab{bu}"],
                note=f"s{s}ci{ci}:vld")
            k_rdy[ci] = [(KVS[bu], KVS[bu].n)]

        def pump(ci_cur):
            while next_load[0] < len(cons) and next_load[0] <= ci_cur + 3:
                emit_slab_load(next_load[0])
                next_load[0] += 1

        pump(-1)
        ci_base = 0
        for j in range(NT):
            tok = slice(j * TILE, (j + 1) * TILE)
            nkt = 4 * (j + 1)
            xa, xar = a2_x[j]
            tg = f"A2s{s}q{j}"

            for hb in range(2):
                iters = [(kt_i, hh) for kt_i in range(nkt) for hh in range(8)]
                n_it = len(iters)
                base = ci_base

                def emit_S(k):
                    kt_i, hh = iters[k]
                    ci = base + kt_i
                    bu = ci % 4
                    h = 8 * hb + hh
                    sa, sb = 2 * (k % 2), 2 * (k % 2) + 1
                    mm_group([(kslab[bu][0:64, hh, :], q12_s[0:64, h, :])],
                             pbank[sa][:, :],
                             reads=[f"kslab{bu}", f"q12_{h}"],
                             writes=[f"pb{sa}"], extra=k_rdy[ci],
                             note=f"{tg}b{hb}k{kt_i}h{hh}:S1")
                    mm_group([(kslab[bu][64:128, hh, :],
                               q12_s[64:128, h, :])],
                             pbank[sb][:, :],
                             reads=[f"kslab{bu}", f"q12_{h}"],
                             writes=[f"pb{sb}"],
                             note=f"{tg}b{hb}k{kt_i}h{hh}:S2")

                def emit_chain(k):
                    kt_i, hh = iters[k]
                    ci = base + kt_i
                    bu = ci % 4
                    sa, sb = 2 * (k % 2), 2 * (k % 2) + 1
                    pbuf = k % 2
                    diag = kt_i >= 4 * j
                    act(s1s[:, :], pbank[sa][:, :], AF.Copy,
                        scale=1.0 / 4096.0,
                        reads=[f"pb{sa}"], writes=["s1s"],
                        note=f"{tg}b{hb}k{kt_i}h{hh}:ev")
                    tt(p_s[:, pbuf, :], pbank[sb][:, :], s1s[:, :],
                       ALU.mult, reads=[f"pb{sb}", "s1s"],
                       writes=[f"p{pbuf}"],
                       note=f"{tg}b{hb}k{kt_i}h{hh}:P")
                    if diag:
                        moff = kt_i - 4 * j

                        def mfn(g, o=p_s[:, pbuf, :], base2=-128 * moff):
                            return g.affine_select(
                                out=o, in_=o, pattern=[[1, TILE]],
                                compare_op=ALU.is_ge, fill=0.0,
                                base=base2, channel_multiplier=-1)
                        P.ev("gpsimd", mfn, reads=[f"p{pbuf}"],
                             writes=[f"p{pbuf}"],
                             note=f"{tg}b{hb}k{kt_i}h{hh}:mask")
                    hh2 = iters[k][1]
                    pt = pair0 if hh2 < 4 else pair1
                    colh = (hh2 % 4) // 2
                    row0 = 64 * (hh2 % 2)
                    yres = f"pair{hh2 // 4}{'ab'[colh]}"
                    mm_group(
                        [(vslab[bu][:, 64 * hh2:64 * hh2 + 64],
                          p_s[:, pbuf, :])],
                        pt[row0:row0 + 64, 512 * colh:512 * (colh + 1)],
                        reads=[f"p{pbuf}", f"vslab{bu}"],
                        writes=[yres], extra=k_rdy[ci],
                        note=f"{tg}b{hb}k{kt_i}h{hh2}:PV",
                        start=(kt_i == 0), stop=(kt_i == nkt - 1),
                        skip=True)
                    if hh2 == 7:
                        pump(ci + 1)

                for k in range(n_it + 2):
                    if k >= 2:
                        emit_chain(k - 2)
                    if k < n_it:
                        emit_S(k)

                for hh in range(8):
                    h = 8 * hb + hh
                    pt = pair0 if hh < 4 else pair1
                    colh = (hh % 4) // 2
                    row0 = 64 * (hh % 2)
                    yres = f"pair{hh // 4}{'ab'[colh]}"

                    def yfn(sc, o=yfm[row0:row0 + 64, h // 2, :],
                            i=pt[row0:row0 + 64, 512 * colh:512 * (colh + 1)]):
                        return sc.copy(o, i)
                    P.ev("scalar", yfn, reads=[yres],
                         writes=[f"yfm{h // 2}_{row0}"],
                         note=f"{tg}b{hb}:yev{hh}")
                ci_base += nkt

            if j + 1 < NT:
                a2_prep(j + 1)

            for oc in range(CH):
                pa = oc % 4
                mm_group([(awt[:, dc, 2048 + 128 * oc:2048 + 128 * (oc + 1)],
                           yfm[:, dc, :]) for dc in range(CH)],
                         pbank[pa][:, :],
                         reads=[f"awtB{oc // 4}"]
                               + [f"yfm{dc}_0" for dc in range(CH)]
                               + [f"yfm{dc}_64" for dc in range(CH)],
                         writes=[f"pb{pa}"],
                         note=f"{tg}:wo{oc}")
                act(out_s[:, oc, :], pbank[pa][:, :], AF.Copy,
                    reads=[f"pb{pa}"], writes=["out_s"],
                    note=f"{tg}:woev{oc}")
            emit_ln_residual(xa, xar, f"{tg}:ln3")
            dma("sync", sp_x2[s, :, :, tok], out_s[:, :, :], SX2,
                reads=["out_s"], writes=[f"sp_x2_{s}_{j}"],
                note=f"{tg}:x2dma")
            pace_spill(2)

    finish_spills()

    bar_idx = len(P.events)
    bar_waits = [(CT, CT.n), (CS, CS.n), (CV, CV.n), (CG, CG.n),
                 (SX2, SX2.n)]

    a_stack.close()
    gten = E(nc.sbuf_tensor("gten", [128, HCH, TILE], BF16))
    uten = E(nc.sbuf_tensor("uten", [128, HCH, TILE], BF16))
    hslab = [E(nc.sbuf_tensor(f"hslab{i}", [128, HCH, 128], BF16))
             for i in range(2)]
    ab0 = E(nc.sbuf_tensor("ab0", [128, TILE], BF16))
    ab1 = E(nc.sbuf_tensor("ab1", [128, TILE], BF16))

    NB = SEQ * NT

    def b_parity(n):
        return (gt[0] + n) % 2

    def b_ld_ln(n):
        s2, j2 = n // NT, n % NT
        tok2 = slice(j2 * TILE, (j2 + 1) * TILE)
        gp = b_parity(n)
        xa, xar = x_s[gp], f"x_s{gp}"
        xn, xnf = xn_s[gp], "xn" + str(gp) + "_{}"
        dma("sync", xa[:, :, :], sp_x2[s2, :, :, tok2], SX,
            reads=[f"sp_x2_{s2}_{j2}"], writes=[xar],
            note=f"B{n}:x2ld")
        emit_ln(xa, xar, xn, xnf, f"B{n}:ln2", extra_first=[(SX, SX.n)])

    b_ld_ln(0)

    for n in range(NB):
        s2, j2 = n // NT, n % NT
        gp = b_parity(n)
        xa, xar = x_s[gp], f"x_s{gp}"
        xn, xnf = xn_s[gp], "xn" + str(gp) + "_{}"
        tg = f"B{n}"

        for grp in range(8):
            sl1, sl2 = (2 * grp) % 6, (2 * grp + 1) % 6
            dma("gpsimd", awt[:, :, 512 * sl1:512 * (sl1 + 1)], sp_w1[grp],
                WSL[sl1], reads=[f"sp_w1p{grp // 4}"], writes=[f"wsl{sl1}"],
                note=f"{tg}:w1ld{grp}")
            dma("gpsimd", awt[:, :, 512 * sl2:512 * (sl2 + 1)], sp_w2[grp],
                WSL[sl2], reads=[f"sp_w2p{grp // 4}"], writes=[f"wsl{sl2}"],
                note=f"{tg}:w2ld{grp}")
            r1 = [(WSL[sl1], WSL[sl1].n)]
            r2 = [(WSL[sl2], WSL[sl2].n)]
            for q in range(4):
                oc = 4 * grp + q
                pa, pb2 = 2 * (oc % 2), 2 * (oc % 2) + 1
                mm_group([(awt[:, ic,
                               512 * sl1 + 128 * q:512 * sl1 + 128 * (q + 1)],
                           xn[:, ic, :]) for ic in range(CH)],
                         pbank[pa][:, :],
                         reads=[f"wsl{sl1}"] + [xnf.format(ic)
                                                for ic in range(CH)],
                         writes=[f"pb{pa}"], extra=r1,
                         note=f"{tg}:w1mm{oc}")
                mm_group([(awt[:, ic,
                               512 * sl2 + 128 * q:512 * sl2 + 128 * (q + 1)],
                           xn[:, ic, :]) for ic in range(CH)],
                         pbank[pb2][:, :],
                         reads=[f"wsl{sl2}"] + [xnf.format(ic)
                                                for ic in range(CH)],
                         writes=[f"pb{pb2}"], extra=r2,
                         note=f"{tg}:w2mm{oc}")
                act(ab0[:, :], pbank[pa][:, :], AF.Silu,
                    scale=scl_s["s1v"][:, oc:oc + 1],
                    reads=[f"pb{pa}", "s1v"], writes=["ab0"],
                    note=f"{tg}:silu{oc}")
                act(ab1[:, :], pbank[pb2][:, :], AF.Copy,
                    scale=scl_s["s2v"][:, oc:oc + 1],
                    reads=[f"pb{pb2}", "s2v"], writes=["ab1"],
                    note=f"{tg}:w2ev{oc}")
                tt(gten[:, oc, :], ab0[:, :], ab1[:, :], ALU.mult,
                   reads=["ab0", "ab1"], writes=[f"g{oc}"],
                   note=f"{tg}:g{oc}")

        for oc in range(HCH):
            hb2 = oc % 2
            dma("gpsimd", hslab[hb2][:, :, :], sp_h1[oc], HSL[hb2],
                reads=[f"sp_h1p{oc // 4}"], writes=[f"hslab{hb2}"],
                note=f"{tg}:h1ld{oc}")
            hr = [(HSL[hb2], HSL[hb2].n)]
            pa = oc % 4
            mm_group([(hslab[hb2][:, ic, :], gten[:, ic, :])
                      for ic in range(HCH)],
                     pbank[pa][:, :],
                     reads=[f"hslab{hb2}"] + [f"g{ic}" for ic in range(HCH)],
                     writes=[f"pb{pa}"], extra=hr, note=f"{tg}:h1mm{oc}")
            act(uten[:, oc, :], pbank[pa][:, :], AF.Gelu,
                scale=scl_s["s3v"][:, oc:oc + 1],
                reads=[f"pb{pa}", "s3v"], writes=[f"u{oc}"],
                note=f"{tg}:gelu1_{oc}")

        if n + 1 < NB:
            b_ld_ln(n + 1)

        for oc in range(HCH):
            hb2 = oc % 2
            dma("gpsimd", hslab[hb2][:, :, :], sp_h2[oc], HSL[hb2],
                reads=[f"sp_h2p{oc // 4}"], writes=[f"hslab{hb2}"],
                note=f"{tg}:h2ld{oc}")
            hr = [(HSL[hb2], HSL[hb2].n)]
            pa = oc % 4
            mm_group([(hslab[hb2][:, ic, :], uten[:, ic, :])
                      for ic in range(HCH)],
                     pbank[pa][:, :],
                     reads=[f"hslab{hb2}"] + [f"u{ic}" for ic in range(HCH)],
                     writes=[f"pb{pa}"], extra=hr, note=f"{tg}:h2mm{oc}")
            act(gten[:, oc, :], pbank[pa][:, :], AF.Gelu,
                scale=scl_s["s4v"][:, oc:oc + 1],
                reads=[f"pb{pa}", "s4v"], writes=[f"g{oc}"],
                note=f"{tg}:gelu2_{oc}")

        for oc in range(CH):
            hb2 = oc % 2
            dma("gpsimd", hslab[hb2][:, :, :], sp_w3[oc], HSL[hb2],
                reads=[f"sp_w3p{oc // 4}"], writes=[f"hslab{hb2}"],
                note=f"{tg}:w3ld{oc}")
            hr = [(HSL[hb2], HSL[hb2].n)]
            pa = oc % 4
            mm_group([(hslab[hb2][:, ic, :], gten[:, ic, :])
                      for ic in range(HCH)],
                     pbank[pa][:, :],
                     reads=[f"hslab{hb2}"] + [f"g{ic}" for ic in range(HCH)],
                     writes=[f"pb{pa}"], extra=hr, note=f"{tg}:w3mm{oc}")
            act(out_s[:, oc, :], pbank[pa][:, :], AF.Copy,
                scale=scl_s["s5v"][:, oc:oc + 1],
                reads=[f"pb{pa}", "s5v"], writes=["out_s"],
                note=f"{tg}:w3ev{oc}")

        emit_ln_residual(xa, xar, f"{tg}:ln4")
        dma("sync", dy[:, :, s2 * T + j2 * TILE:s2 * T + (j2 + 1) * TILE],
            out_s[:, :, :], SO, reads=["out_s"], writes=[],
            note=f"{tg}:ydma")

    seen_eng = set()
    for i in range(bar_idx, len(P.events)):
        e = P.events[i]
        if e["eng"] in seen_eng:
            continue
        seen_eng.add(e["eng"])
        own = P.eng_ctr.get(e["eng"])
        merged = {sm: v for sm, v in e["waits"]}
        for sm, v in bar_waits:
            if sm is own or v <= 0:
                continue
            if merged.get(sm, 0) < v:
                merged[sm] = v
        e["waits"] = list(merged.items())

    P.check()

    with nc.Block() as block:
        def run_engine(eng_obj, name):
            seen = {}
            for e2 in P.events:
                if e2["eng"] != name:
                    continue
                for s2_, v in e2["waits"]:
                    if seen.get(s2_, 0) >= v:
                        continue
                    seen[s2_] = v
                    eng_obj.wait_ge(s2_.h, v)
                ins = e2["fn"](eng_obj)
                s2_, _v = e2["inc"]
                ins.then_inc(s2_.h, 16 if s2_.dma else 1)
            if name == "sync":
                eng_obj.wait_ge(SO.h, 16 * SEQ * NT)

        @block.tensor
        def _(t):
            run_engine(t, "tensor")

        @block.scalar
        def _(sc):
            run_engine(sc, "scalar")

        @block.vector
        def _(v):
            run_engine(v, "vector")

        @block.gpsimd
        def _(g):
            run_engine(g, "gpsimd")

        @block.sync
        def _(sy):
            run_engine(sy, "sync")

    stack.close()
    return nc



def kernel(**inputs):
    if "nc" not in _CACHED:
        _CACHED["nc"] = _build()
    nc = _CACHED["nc"]

    w = _pack_weights(inputs)
    consts = _host_consts()
    xs = _pack_x(np.asarray(inputs["L0_attn_x"], np.float32))

    base = {}
    base.update(w)
    base.update(consts)
    in_maps = []
    for cid in range(NCORE):
        m = dict(base)
        m["x"] = xs[cid]
        in_maps.append(m)

    res = run_bass_kernel_spmd(nc, in_maps, core_ids=list(range(NCORE)))
    return _unpack_y([r["y"] for r in res.results])


# version 08
# speedup vs baseline: 1.1281x; 1.0235x over previous
import numpy as np
import ml_dtypes
import concourse.bass as bass
import concourse.mybir as mybir
from concourse.bass_utils import run_bass_kernel_spmd

F32 = mybir.dt.float32
BF16 = mybir.dt.bfloat16
AF = mybir.ActivationFunctionType
ALU = mybir.AluOpType
bfdt = ml_dtypes.bfloat16

B, T, C = 16, 2048, 1024
H, D = 16, 64
HID = 4096
NCORE = 8
SEQ = 2
TILE = 512
NT = T // TILE
CH = C // 128
HCH = HID // 128
KT = T // 128
EPS = 1e-5

_CACHED = {}



def _pack_weights(inp):
    w = {}

    def cat(p1, p2):
        a = np.asarray(p1).reshape(H, 64, C)
        b = np.asarray(p2).reshape(H, 64, C)
        return np.stack([a, b], axis=1).reshape(2 * C, C)

    def to_sb(mat, och):
        t = np.asarray(mat).T.reshape(CH, 128, och * 128).transpose(1, 0, 2)
        return np.ascontiguousarray(t).astype(np.float32)

    w["wkcat"] = to_sb(cat(inp["L0_attn_Wk1"], inp["L0_attn_Wk2"]), 16)
    w["wqcat"] = to_sb(cat(inp["L0_attn_Wq1"], inp["L0_attn_Wq2"]), 16)
    w["wv"] = to_sb(inp["L0_attn_Wv"], 8)
    w["wout"] = to_sb(inp["L0_attn_Wout"], 8)

    def ffn_chunks(mat):
        mat = np.asarray(mat)
        o, i = mat.shape
        noc, nic = o // 128, i // 128
        t = mat.T.reshape(nic, 128, noc, 128).transpose(2, 1, 0, 3)
        return np.ascontiguousarray(t).astype(np.float32)

    def ffn_groups(mat):
        mat = np.asarray(mat)
        t = mat.T.reshape(CH, 128, 8, 512).transpose(2, 1, 0, 3)
        return np.ascontiguousarray(t).astype(np.float32)

    w["w1src"] = ffn_groups(inp["L0_ffn_w1_w"])
    w["w2src"] = ffn_groups(inp["L0_ffn_w2_w"])
    w["h1src"] = ffn_chunks(inp["L0_ffn_h1_w"])
    w["h2src"] = ffn_chunks(inp["L0_ffn_h2_w"])
    w["w3src"] = ffn_chunks(inp["L0_ffn_w3_w"])

    def scl(v):
        return np.ascontiguousarray(np.asarray(v).reshape(-1, 128).T).astype(
            np.float32)

    w["s1v"] = scl(inp["L0_ffn_w1_s"])
    w["s2v"] = scl(inp["L0_ffn_w2_s"])
    w["s3v"] = scl(inp["L0_ffn_h1_s"])
    w["s4v"] = scl(inp["L0_ffn_h2_s"])
    w["s5v"] = scl(inp["L0_ffn_w3_s"])
    return w


def _host_consts():
    c = {}
    inv = 1.0 / (10000.0 ** (np.arange(0, D, 2, dtype=np.float32) / D))
    freqs = np.arange(T, dtype=np.float32)[:, None] * inv[None, :]
    cos = np.cos(freqs).astype(bfdt).astype(np.float32).T
    sin = np.sin(freqs).astype(bfdt).astype(np.float32).T
    c["ctab"] = np.ascontiguousarray(np.tile(cos, (4, 1)))
    c["stab"] = np.ascontiguousarray(
        np.concatenate([-sin, sin, -sin, sin], axis=0))
    m = np.arange(128)
    swp = np.where((m % 64) < 32, m + 32, m - 32)
    S = np.zeros((128, 128), np.float32)
    S[swp, m] = 1.0
    c["swapm"] = S
    c["onesm"] = np.full((128, 128), 1.0 / C, np.float32)
    return c


def _pack_x(x):
    out = []
    for cid in range(NCORE):
        xc = x[2 * cid:2 * cid + 2]
        t = xc.transpose(0, 2, 1).reshape(SEQ, CH, 128, T).transpose(0, 2, 1, 3)
        out.append(np.ascontiguousarray(t).astype(np.float32))
    return out


def _unpack_y(parts):
    outs = []
    for arr in parts:
        a = arr.reshape(128, CH, SEQ, T).transpose(2, 3, 1, 0)
        outs.append(a.reshape(SEQ, T, C))
    return np.ascontiguousarray(np.concatenate(outs, axis=0))



class _Sem:
    __slots__ = ("h", "n", "name", "dma")

    def __init__(self, h, name, dma=False):
        self.h = h
        self.n = 0
        self.name = name
        self.dma = dma


class _Prog:
    def __init__(self):
        self.events = []
        self.writes = {}
        self.guards = {}
        self.eng_ctr = {}

    def ev(self, eng, fn, reads=(), writes=(), extra=(), dma_sem=None, note=""):
        waits = {}

        def add(pair):
            if pair is None:
                return
            s, v = pair
            if v <= 0:
                return
            if waits.get(s, 0) < v:
                waits[s] = v

        for r in reads:
            add(self.writes.get(r))
        for r in writes:
            for p in self.guards.get(r, ()):
                add(p)
            add(self.writes.get(r))
        for p in extra:
            add(p)

        if dma_sem is not None:
            dma_sem.n += 16
            inc = (dma_sem, dma_sem.n)
        else:
            ctr = self.eng_ctr[eng]
            ctr.n += 1
            inc = (ctr, ctr.n)

        own = self.eng_ctr.get(eng)
        wl = [(s, v) for s, v in waits.items() if s is not own]

        self.events.append(dict(eng=eng, fn=fn, waits=wl, inc=inc, note=note))
        for r in reads:
            self.guards.setdefault(r, []).append(inc)
        for r in writes:
            self.writes[r] = inc
            self.guards[r] = []
        return inc

    def check(self):
        eng_names = sorted({e["eng"] for e in self.events})
        queues = {n: [e for e in self.events if e["eng"] == n]
                  for n in eng_names}
        ptr = {n: 0 for n in eng_names}
        val = {}
        total = len(self.events)
        done = 0
        progress = True
        while progress:
            progress = False
            for n in eng_names:
                while ptr[n] < len(queues[n]):
                    e = queues[n][ptr[n]]
                    if all(val.get(s, 0) >= v for s, v in e["waits"]):
                        s, v = e["inc"]
                        val[s] = max(val.get(s, 0), v)
                        ptr[n] += 1
                        done += 1
                        progress = True
                    else:
                        break
        if done != total:
            lines = []
            for n in eng_names:
                if ptr[n] < len(queues[n]):
                    e = queues[n][ptr[n]]
                    missing = [(s.name, v, val.get(s, 0))
                               for s, v in e["waits"] if val.get(s, 0) < v]
                    lines.append(f"  {n}@{ptr[n]}/{len(queues[n])} "
                                 f"'{e['note']}' missing {missing}")
            raise RuntimeError("schedule deadlock:\n" + "\n".join(lines))



def _build(dbg=False):
    from contextlib import ExitStack

    nc = bass.Bass("TRN2", target_bir_lowering=False, debug=False)
    P = _Prog()

    dx = nc.dram_tensor("x", [SEQ, 128, CH, T], F32, kind="ExternalInput").ap()
    dwk = nc.dram_tensor("wkcat", [128, CH, 2048], F32, kind="ExternalInput").ap()
    dwq = nc.dram_tensor("wqcat", [128, CH, 2048], F32, kind="ExternalInput").ap()
    dwv = nc.dram_tensor("wv", [128, CH, 1024], F32, kind="ExternalInput").ap()
    dwo = nc.dram_tensor("wout", [128, CH, 1024], F32, kind="ExternalInput").ap()
    dct = nc.dram_tensor("ctab", [128, T], F32, kind="ExternalInput").ap()
    dstb = nc.dram_tensor("stab", [128, T], F32, kind="ExternalInput").ap()
    dsw = nc.dram_tensor("swapm", [128, 128], F32, kind="ExternalInput").ap()
    don = nc.dram_tensor("onesm", [128, 128], F32, kind="ExternalInput").ap()
    dw1 = nc.dram_tensor("w1src", [8, 128, CH, 512], F32, kind="ExternalInput").ap()
    dw2 = nc.dram_tensor("w2src", [8, 128, CH, 512], F32, kind="ExternalInput").ap()
    dh1 = nc.dram_tensor("h1src", [HCH, 128, HCH, 128], F32, kind="ExternalInput").ap()
    dh2 = nc.dram_tensor("h2src", [HCH, 128, HCH, 128], F32, kind="ExternalInput").ap()
    dw3 = nc.dram_tensor("w3src", [CH, 128, HCH, 128], F32, kind="ExternalInput").ap()
    dscl = {}
    for nme, cols in (("s1v", HCH), ("s2v", HCH), ("s3v", HCH), ("s4v", HCH),
                      ("s5v", CH)):
        dscl[nme] = nc.dram_tensor(nme, [128, cols], F32,
                                   kind="ExternalInput").ap()
    dy = nc.dram_tensor("y", [128, CH, SEQ * T], F32, kind="ExternalOutput").ap()

    sp_wk = nc.dram_tensor("sp_wk", [128, CH, 2048], BF16, kind="Internal").ap()
    sp_wq = nc.dram_tensor("sp_wq", [128, CH, 2048], BF16, kind="Internal").ap()
    sp_wv = nc.dram_tensor("sp_wv", [128, CH, 1024], BF16, kind="Internal").ap()
    sp_wo = nc.dram_tensor("sp_wo", [128, CH, 1024], BF16, kind="Internal").ap()
    sp_w1 = nc.dram_tensor("sp_w1", [8, 128, CH, 512], BF16, kind="Internal").ap()
    sp_w2 = nc.dram_tensor("sp_w2", [8, 128, CH, 512], BF16, kind="Internal").ap()
    sp_h1 = nc.dram_tensor("sp_h1", [HCH, 128, HCH, 128], BF16, kind="Internal").ap()
    sp_h2 = nc.dram_tensor("sp_h2", [HCH, 128, HCH, 128], BF16, kind="Internal").ap()
    sp_w3 = nc.dram_tensor("sp_w3", [CH, 128, HCH, 128], BF16, kind="Internal").ap()
    kdbg = "ExternalOutput" if dbg else "Internal"
    sp_k = nc.dram_tensor("sp_k", [NT, 128, 4, 16, 128], BF16, kind=kdbg).ap()
    sp_v = nc.dram_tensor("sp_v", [KT, 128, 1024], BF16, kind=kdbg).ap()
    sp_x2 = nc.dram_tensor("sp_x2", [SEQ, 128, CH, T], F32,
                           kind="ExternalOutput" if dbg else "Internal").ap()

    stack = ExitStack()
    E = stack.enter_context

    def sem(name, dma=False):
        return _Sem(E(nc.semaphore(name)), name, dma)

    SW = sem("SW", True)
    SC = sem("SC", True)
    SPL = sem("SPL", True)
    SX = sem("SX", True)
    SKK = sem("SKK", True)
    SKV = [sem("SKV0", True), sem("SKV1", True)]
    KVS = [sem(f"KVS{i}", True) for i in range(4)]
    HSL = [sem("HSL0", True), sem("HSL1", True)]
    WSL = [sem(f"WSL{i}", True) for i in range(6)]
    SX2 = sem("SX2", True)
    SO = sem("SO", True)
    CT = sem("CT")
    CS = sem("CS")
    CV = sem("CV")
    CG = sem("CG")
    P.eng_ctr = {"tensor": CT, "scalar": CS, "vector": CV, "gpsimd": CG}

    awt = E(nc.sbuf_tensor("awt", [128, CH, 3072], BF16))
    x_s = [E(nc.sbuf_tensor(f"x_s{i}", [128, CH, TILE], F32)) for i in range(2)]
    xn_s = [E(nc.sbuf_tensor(f"xn_s{i}", [128, CH, TILE], BF16))
            for i in range(2)]
    cbf = [E(nc.sbuf_tensor(f"cbf{i}", [128, TILE], BF16)) for i in range(2)]
    csq = [E(nc.sbuf_tensor(f"csq{i}", [128, TILE], BF16)) for i in range(2)]
    mean_s = E(nc.sbuf_tensor("mean_s", [128, TILE], F32))
    rstd_s = E(nc.sbuf_tensor("rstd_s", [128, TILE], F32))
    vtmp = E(nc.sbuf_tensor("vtmp", [128, TILE], F32))
    t1_s = E(nc.sbuf_tensor("t1_s", [128, TILE], F32))
    out_s = E(nc.sbuf_tensor("out_s", [128, CH, TILE], F32))
    swap_s = E(nc.sbuf_tensor("swap_s", [128, 128], BF16))
    ones_s = E(nc.sbuf_tensor("ones_s", [128, 128], BF16))
    epsb = E(nc.sbuf_tensor("epsb", [128, 1], F32))
    scl_s = {}
    for nme, cols in (("s1v", HCH), ("s2v", HCH), ("s3v", HCH), ("s4v", HCH),
                      ("s5v", CH)):
        scl_s[nme] = E(nc.sbuf_tensor("sb_" + nme, [128, cols], F32))

    a_stack = ExitStack()
    A_ = a_stack.enter_context
    t2_s = A_(nc.sbuf_tensor("t2_s", [128, TILE], BF16))
    ctab_s = A_(nc.sbuf_tensor("ctab_s", [128, T], F32))
    stab_s = A_(nc.sbuf_tensor("stab_s", [128, T], F32))
    k12st = A_(nc.sbuf_tensor("k12st", [128, 4, 16, 128], BF16))
    vstage = [A_(nc.sbuf_tensor(f"vstage{i}", [128, 1024], BF16))
              for i in range(2)]
    q12_s = A_(nc.sbuf_tensor("q12_s", [128, 16, TILE], BF16))
    kslab = [A_(nc.sbuf_tensor(f"kslab{i}", [128, 8, 128], BF16))
             for i in range(4)]
    vslab = [A_(nc.sbuf_tensor(f"vslab{i}", [128, TILE], BF16))
             for i in range(4)]
    s1s = A_(nc.sbuf_tensor("s1s", [128, TILE], F32))
    p_s = A_(nc.sbuf_tensor("p_s", [128, 2, TILE], BF16))
    yfm = A_(nc.sbuf_tensor("yfm", [128, CH, TILE], BF16))

    pair0 = E(nc.psum_tensor("pair0", [128, 1024], F32))
    pair1 = E(nc.psum_tensor("pair1", [128, 1024], F32))
    pbank = [E(nc.psum_tensor(f"pb{i}", [128, 512], F32)) for i in range(4)]

    def dma(eng, out_ap, in_ap, s, reads=(), writes=(), extra=(), note=""):
        def fn(e, o=out_ap, i=in_ap):
            return e.dma_start(out=o, in_=i)
        return P.ev(eng, fn, reads=reads, writes=writes, extra=extra,
                    dma_sem=s, note=note)

    def mm_group(pairs, out_ap, reads, writes, extra=(), note="", start=True,
                 stop=True, skip=False):
        def fn(t, pr=tuple(pairs), o=out_ap, st=start, sp=stop, sk=skip):
            ins = None
            n = len(pr)
            for i, (l, r) in enumerate(pr):
                ins = t.matmul(o, l, r, start=(st and i == 0),
                               stop=(sp and i == n - 1), skip_group_check=sk)
            return ins
        return P.ev("tensor", fn, reads=reads, writes=writes, extra=extra,
                    note=note)

    def act(out_ap, in_ap, func, reads, writes, scale=1.0, bias=0.0, extra=(),
            note=""):
        def fn(s, o=out_ap, i=in_ap, f=func, sc=scale, b=bias):
            return s.activation(o, i, f, scale=sc, bias=b)
        return P.ev("scalar", fn, reads=reads, writes=writes, extra=extra,
                    note=note)

    def tt(out_ap, a_ap, b_ap, op, reads, writes, extra=(), note=""):
        def fn(v, o=out_ap, a=a_ap, b=b_ap, p=op):
            return v.tensor_tensor(o, a, b, p)
        return P.ev("vector", fn, reads=reads, writes=writes, extra=extra,
                    note=note)

    def vcopy(out_ap, in_ap, reads, writes, extra=(), note=""):
        def fn(v, o=out_ap, i=in_ap):
            return v.tensor_copy(o, i)
        return P.ev("vector", fn, reads=reads, writes=writes, extra=extra,
                    note=note)

    def vrecip(out_ap, in_ap, reads, writes, note=""):
        def fn(v, o=out_ap, i=in_ap):
            return v.reciprocal(o, i)
        return P.ev("vector", fn, reads=reads, writes=writes, note=note)

    def r3(ap):
        return ap.rearrange("p (m t) -> p m t", m=4)

    def emit_ln_stats(src, src_res, tag, extra_first=()):
        for c in range(CH):
            pp = c % 2
            vcopy(cbf[pp][:, :], src[:, c, :],
                  reads=[src_res], writes=[f"cbf{pp}"],
                  extra=extra_first if c == 0 else (),
                  note=f"{tag}:cast{c}")
            tt(csq[pp][:, :], cbf[pp][:, :], cbf[pp][:, :], ALU.mult,
               reads=[f"cbf{pp}"], writes=[f"csq{pp}"], note=f"{tag}:sq{c}")
            mm_group([(ones_s[:, :], cbf[pp][:, :])], pbank[0][:, :],
                     reads=[f"cbf{pp}", "ones_s"],
                     writes=["pb0"] if c == 0 else [],
                     extra=() if c == 0 else [P.writes.get("pb0")],
                     note=f"{tag}:mmm{c}", start=(c == 0), stop=(c == CH - 1))
            if c != 0:
                P.writes["pb0"] = P.events[-1]["inc"]
            mm_group([(ones_s[:, :], csq[pp][:, :])], pbank[1][:, :],
                     reads=[f"csq{pp}", "ones_s"],
                     writes=["pb1"] if c == 0 else [],
                     extra=() if c == 0 else [P.writes.get("pb1")],
                     note=f"{tag}:mms{c}", start=(c == 0), stop=(c == CH - 1))
            if c != 0:
                P.writes["pb1"] = P.events[-1]["inc"]
        act(mean_s[:, :], pbank[0][:, :], AF.Copy, reads=["pb0"],
            writes=["mean_s"], note=f"{tag}:mean_ev")
        tt(vtmp[:, :], mean_s[:, :], mean_s[:, :], ALU.mult,
           reads=["mean_s"], writes=["vtmp"], note=f"{tag}:mean2")
        tt(vtmp[:, :], pbank[1][:, :], vtmp[:, :], ALU.subtract,
           reads=["pb1"], writes=["vtmp"], note=f"{tag}:var")
        act(t1_s[:, :], vtmp[:, :], AF.Sqrt, scale=1.0, bias=epsb[:, 0:1],
            reads=["vtmp", "epsb"], writes=["t1_s"], note=f"{tag}:sqrt")
        vrecip(rstd_s[:, :], t1_s[:, :], reads=["t1_s"], writes=["rstd_s"],
               note=f"{tag}:recip")

    def emit_ln(src, src_res, xn, xn_fmt, tag, extra_first=()):
        emit_ln_stats(src, src_res, tag, extra_first)
        for c in range(CH):
            tt(t1_s[:, :], src[:, c, :], mean_s[:, :], ALU.subtract,
               reads=[src_res, "mean_s"], writes=["t1_s"], note=f"{tag}:sub{c}")
            tt(xn[:, c, :], t1_s[:, :], rstd_s[:, :], ALU.mult,
               reads=["t1_s", "rstd_s"], writes=[xn_fmt.format(c)],
               note=f"{tag}:nrm{c}")

    def emit_ln_residual(xr, xr_res, tag):
        emit_ln_stats(out_s, "out_s", tag)
        for c in range(CH):
            tt(t1_s[:, :], out_s[:, c, :], mean_s[:, :], ALU.subtract,
               reads=["out_s", "mean_s"], writes=["t1_s"], note=f"{tag}:sub{c}")
            tt(vtmp[:, :], t1_s[:, :], rstd_s[:, :], ALU.mult,
               reads=["t1_s", "rstd_s"], writes=["vtmp"], note=f"{tag}:nrm{c}")
            tt(out_s[:, c, :], vtmp[:, :], xr[:, c, :], ALU.add,
               reads=["vtmp", xr_res], writes=["out_s"], note=f"{tag}:res{c}")

    def emit_rope(dst_fn, wrap3, xn, xn_fmt, ct_sl, st_sl, tag):
        def proj(c):
            pa = c % 2
            mm_group([(awt[:, ic, 128 * c:128 * (c + 1)], xn[:, ic, :])
                      for ic in range(CH)],
                     pbank[pa][:, :],
                     reads=[f"awtA{c // 4}"] + [xn_fmt.format(ic)
                                               for ic in range(CH)],
                     writes=[f"pb{pa}"], note=f"{tag}:proj{c}")

        def tail(c):
            pa, pb = c % 2, 2 + c % 2
            tt(t1_s[:, :], pbank[pa][:, :], ct_sl, ALU.mult,
               reads=[f"pb{pa}", "ctab"], writes=["t1_s"], note=f"{tag}:ct{c}")
            tt(t2_s[:, :], pbank[pa][:, :], st_sl, ALU.mult,
               reads=[f"pb{pa}", "stab"], writes=["t2_s"], note=f"{tag}:st{c}")
            mm_group([(swap_s[:, :], t2_s[:, :])], pbank[pb][:, :],
                     reads=["t2_s", "swap_s"], writes=[f"pb{pb}"],
                     note=f"{tag}:swap{c}")
            dap, dres = dst_fn(c)
            if wrap3:
                tt(dap, r3(pbank[pb][:, :]), r3(t1_s[:, :]), ALU.add,
                   reads=[f"pb{pb}", "t1_s"], writes=[dres],
                   note=f"{tag}:rot{c}")
            else:
                tt(dap, pbank[pb][:, :], t1_s[:, :], ALU.add,
                   reads=[f"pb{pb}", "t1_s"], writes=[dres],
                   note=f"{tag}:rot{c}")

        for c in range(17):
            if c < 16:
                proj(c)
            if c >= 1:
                tail(c - 1)

    def msfn(v, o=epsb[:, 0:1], cc=EPS):
        return v.memset(o, cc)
    P.ev("vector", msfn, writes=["epsb"], note="memset eps")

    dma("gpsimd", ones_s[:, :], don, SC, writes=["ones_s"], note="ld ones")
    dma("gpsimd", swap_s[:, :], dsw, SC, writes=["swap_s"], note="ld swap")
    dma("scalar", ctab_s[:, :], dct, SC, writes=["ctab"], note="ld ctab")
    dma("scalar", stab_s[:, :], dstb, SC, writes=["stab"], note="ld stab")
    for nme in scl_s:
        dma("scalar", scl_s[nme][:, :], dscl[nme], SC, writes=[nme],
            note=f"ld {nme}")

    for q in range(4):
        dma("gpsimd", awt[:, :, 512 * q:512 * (q + 1)],
            dwk[:, :, 512 * q:512 * (q + 1)], SW,
            writes=[f"awtA{q}"], note=f"wk0:A{q}")
    for hh in range(2):
        dma("gpsimd", awt[:, :, 2048 + 512 * hh:2048 + 512 * (hh + 1)],
            dwv[:, :, 512 * hh:512 * (hh + 1)], SW,
            writes=[f"awtB{hh}"], note=f"wv0:B{hh}")

    spill_pieces = [(sp_wq, dwq, "sp_wqd"), (sp_wo, dwo, "sp_wod"),
                    (sp_wk, dwk, "sp_wkd"), (sp_wv, dwv, "sp_wvd")]
    for qq in range(2):
        spill_pieces.append((sp_w1[4 * qq:4 * (qq + 1)],
                             dw1[4 * qq:4 * (qq + 1)], f"sp_w1p{qq}"))
        spill_pieces.append((sp_w2[4 * qq:4 * (qq + 1)],
                             dw2[4 * qq:4 * (qq + 1)], f"sp_w2p{qq}"))
        spill_pieces.append((sp_w3[4 * qq:4 * (qq + 1)],
                             dw3[4 * qq:4 * (qq + 1)], f"sp_w3p{qq}"))
    for qq in range(8):
        spill_pieces.append((sp_h1[4 * qq:4 * (qq + 1)],
                             dh1[4 * qq:4 * (qq + 1)], f"sp_h1p{qq}"))
    for qq in range(8):
        spill_pieces.append((sp_h2[4 * qq:4 * (qq + 1)],
                             dh2[4 * qq:4 * (qq + 1)], f"sp_h2p{qq}"))
    spill_i = [0]

    def pace_spill(k=2):
        for _ in range(k):
            if spill_i[0] >= len(spill_pieces):
                return
            d2, s2_, res = spill_pieces[spill_i[0]]
            spill_i[0] += 1
            dma("gpsimd", d2, s2_, SPL, writes=[res], extra=[(CT, CT.n)],
                note=f"spill {res}")

    def finish_spills():
        while spill_i[0] < len(spill_pieces):
            pace_spill(4)

    def load_awtA(src, res, tag):
        for q in range(4):
            dma("scalar", awt[:, :, 512 * q:512 * (q + 1)],
                src[:, :, 512 * q:512 * (q + 1)], SW,
                reads=[res], writes=[f"awtA{q}"], note=f"{tag}:A{q}")

    def load_awtB(src, res, tag):
        for hh in range(2):
            dma("scalar", awt[:, :, 2048 + 512 * hh:2048 + 512 * (hh + 1)],
                src[:, :, 512 * hh:512 * (hh + 1)], SW,
                reads=[res], writes=[f"awtB{hh}"], note=f"{tag}:B{hh}")

    gt = [0]
    x0_pre = [False]

    def x_load(s2, t2, gp2, tag):
        tok2 = slice(t2 * TILE, (t2 + 1) * TILE)
        dma("scalar", x_s[gp2][:, :, :], dx[s2, :, :, tok2], SX,
            writes=[f"x_s{gp2}"], note=f"{tag}:x")

    for s in range(SEQ):
        if not x0_pre[0]:
            x_load(s, 0, gt[0] % 2, f"A1s{s}t0")
        x0_pre[0] = False
        if s > 0:
            load_awtA(sp_wk, "sp_wkd", f"A1s{s}wk")
            load_awtB(sp_wv, "sp_wvd", f"A1s{s}wv")

        for it in range(NT):
            tok = slice(it * TILE, (it + 1) * TILE)
            gp = gt[0] % 2
            gt[0] += 1
            xa, xar = x_s[gp], f"x_s{gp}"
            xn, xnf = xn_s[gp], "xn" + str(gp) + "_{}"
            tg = f"A1s{s}t{it}"
            emit_ln(xa, xar, xn, xnf, tg, extra_first=[(SX, SX.n)])
            if it + 1 < NT:
                x_load(s, it + 1, gt[0] % 2, f"A1s{s}t{it + 1}")
            emit_rope(lambda c: (k12st[:, :, c, :], f"k12st{c}"), True,
                      xn, xnf, ctab_s[:, tok], stab_s[:, tok], tg)
            for m in range(4):
                vb = m % 2
                vpair = pair0 if m % 2 == 0 else pair1
                vres = "pair0" if m % 2 == 0 else "pair1"
                for half in range(2):
                    mm_group(
                        [(xn[:, ic, 128 * m:128 * (m + 1)],
                          awt[:, ic, 2048 + 512 * half:2048 + 512 * (half + 1)])
                         for ic in range(CH)],
                        vpair[:, 512 * half:512 * (half + 1)],
                        reads=[xnf.format(ic) for ic in range(CH)]
                              + [f"awtB{half}"],
                        writes=[f"{vres}{'ab'[half]}"],
                        note=f"{tg}:v{m}h{half}")
                act(vstage[vb][:, :], vpair[:, :], AF.Copy,
                    reads=[f"{vres}a", f"{vres}b"], writes=[f"vstage{vb}"],
                    note=f"{tg}:vev{m}")
                dma("sync", sp_v[4 * it + m], vstage[vb][:, :], SKV[vb],
                    reads=[f"vstage{vb}"], writes=[f"sp_v{4 * it + m}"],
                    note=f"{tg}:vdma{m}")
            dma("sync", sp_k[it], k12st[:, :, :, :], SKK,
                reads=[f"k12st{c}" for c in range(16)],
                writes=[f"sp_kt{it}"], note=f"{tg}:kdump")
            pace_spill(2)

        load_awtA(sp_wq, "sp_wqd", f"A2s{s}wq")
        load_awtB(sp_wo, "sp_wod", f"A2s{s}wo")

        a2_x = {}

        def a2_prep(j):
            tok = slice(j * TILE, (j + 1) * TILE)
            gp = gt[0] % 2
            gt[0] += 1
            xa, xar = x_s[gp], f"x_s{gp}"
            xn, xnf = xn_s[gp], "xn" + str(gp) + "_{}"
            a2_x[j] = (xa, xar)
            tg = f"A2s{s}q{j}"
            x_load(s, j, gp, tg)
            emit_ln(xa, xar, xn, xnf, tg, extra_first=[(SX, SX.n)])
            emit_rope(lambda c: (q12_s[:, c, :], f"q12_{c}"), False,
                      xn, xnf, ctab_s[:, tok], stab_s[:, tok], tg)

        a2_prep(0)

        cons = [(j2, hb2, kt2) for j2 in range(NT) for hb2 in range(2)
                for kt2 in range(4 * (j2 + 1))]
        k_rdy = {}
        next_load = [0]

        def emit_slab_load(ci):
            j2, hb2, kt2 = cons[ci]
            bu = ci % 4
            it2, m2 = divmod(kt2, 4)
            dma("sync", kslab[bu][:, :, :],
                sp_k[it2, :, m2, 8 * hb2:8 * (hb2 + 1), :], KVS[bu],
                reads=[f"sp_kt{it2}"], writes=[f"kslab{bu}"],
                note=f"s{s}ci{ci}:kld")
            dma("sync", vslab[bu][:, :],
                sp_v[kt2][:, 512 * hb2:512 * (hb2 + 1)], KVS[bu],
                reads=[f"sp_v{kt2}"], writes=[f"vslab{bu}"],
                note=f"s{s}ci{ci}:vld")
            k_rdy[ci] = [(KVS[bu], KVS[bu].n)]

        def pump(ci_cur):
            while next_load[0] < len(cons) and next_load[0] <= ci_cur + 3:
                emit_slab_load(next_load[0])
                next_load[0] += 1

        pump(-1)
        ci_base = 0
        for j in range(NT):
            tok = slice(j * TILE, (j + 1) * TILE)
            nkt = 4 * (j + 1)
            xa, xar = a2_x[j]
            tg = f"A2s{s}q{j}"

            for hb in range(2):
                iters = [(kt_i, hh) for kt_i in range(nkt) for hh in range(8)]
                n_it = len(iters)
                base = ci_base

                def emit_S(k):
                    kt_i, hh = iters[k]
                    ci = base + kt_i
                    bu = ci % 4
                    h = 8 * hb + hh
                    sa, sb = 2 * (k % 2), 2 * (k % 2) + 1
                    mm_group([(kslab[bu][0:64, hh, :], q12_s[0:64, h, :])],
                             pbank[sa][:, :],
                             reads=[f"kslab{bu}", f"q12_{h}"],
                             writes=[f"pb{sa}"], extra=k_rdy[ci],
                             note=f"{tg}b{hb}k{kt_i}h{hh}:S1")
                    mm_group([(kslab[bu][64:128, hh, :],
                               q12_s[64:128, h, :])],
                             pbank[sb][:, :],
                             reads=[f"kslab{bu}", f"q12_{h}"],
                             writes=[f"pb{sb}"],
                             note=f"{tg}b{hb}k{kt_i}h{hh}:S2")

                def emit_chain(k):
                    kt_i, hh = iters[k]
                    ci = base + kt_i
                    bu = ci % 4
                    sa, sb = 2 * (k % 2), 2 * (k % 2) + 1
                    pbuf = k % 2
                    diag = kt_i >= 4 * j
                    act(s1s[:, :], pbank[sa][:, :], AF.Copy,
                        scale=1.0 / 4096.0,
                        reads=[f"pb{sa}"], writes=["s1s"],
                        note=f"{tg}b{hb}k{kt_i}h{hh}:ev")
                    tt(p_s[:, pbuf, :], pbank[sb][:, :], s1s[:, :],
                       ALU.mult, reads=[f"pb{sb}", "s1s"],
                       writes=[f"p{pbuf}"],
                       note=f"{tg}b{hb}k{kt_i}h{hh}:P")
                    if diag:
                        moff = kt_i - 4 * j

                        def mfn(g, o=p_s[:, pbuf, :], base2=-128 * moff):
                            return g.affine_select(
                                out=o, in_=o, pattern=[[1, TILE]],
                                compare_op=ALU.is_ge, fill=0.0,
                                base=base2, channel_multiplier=-1)
                        P.ev("gpsimd", mfn, reads=[f"p{pbuf}"],
                             writes=[f"p{pbuf}"],
                             note=f"{tg}b{hb}k{kt_i}h{hh}:mask")
                    hh2 = iters[k][1]
                    pt = pair0 if hh2 < 4 else pair1
                    colh = (hh2 % 4) // 2
                    row0 = 64 * (hh2 % 2)
                    yres = f"pair{hh2 // 4}{'ab'[colh]}"
                    mm_group(
                        [(vslab[bu][:, 64 * hh2:64 * hh2 + 64],
                          p_s[:, pbuf, :])],
                        pt[row0:row0 + 64, 512 * colh:512 * (colh + 1)],
                        reads=[f"p{pbuf}", f"vslab{bu}"],
                        writes=[yres], extra=k_rdy[ci],
                        note=f"{tg}b{hb}k{kt_i}h{hh2}:PV",
                        start=(kt_i == 0), stop=(kt_i == nkt - 1),
                        skip=True)
                    if hh2 == 7:
                        pump(ci + 1)

                for k in range(n_it + 2):
                    if k >= 2:
                        emit_chain(k - 2)
                    if k < n_it:
                        emit_S(k)

                for hh in range(8):
                    h = 8 * hb + hh
                    pt = pair0 if hh < 4 else pair1
                    colh = (hh % 4) // 2
                    row0 = 64 * (hh % 2)
                    yres = f"pair{hh // 4}{'ab'[colh]}"

                    def yfn(sc, o=yfm[row0:row0 + 64, h // 2, :],
                            i=pt[row0:row0 + 64, 512 * colh:512 * (colh + 1)]):
                        return sc.copy(o, i)
                    P.ev("scalar", yfn, reads=[yres],
                         writes=[f"yfm{h // 2}_{row0}"],
                         note=f"{tg}b{hb}:yev{hh}")
                ci_base += nkt

            if j + 1 < NT:
                a2_prep(j + 1)
            elif s + 1 < SEQ:
                x_load(s + 1, 0, gt[0] % 2, f"A1s{s + 1}t0")
                x0_pre[0] = True

            for oc in range(CH):
                pa = oc % 4
                mm_group([(awt[:, dc, 2048 + 128 * oc:2048 + 128 * (oc + 1)],
                           yfm[:, dc, :]) for dc in range(CH)],
                         pbank[pa][:, :],
                         reads=[f"awtB{oc // 4}"]
                               + [f"yfm{dc}_0" for dc in range(CH)]
                               + [f"yfm{dc}_64" for dc in range(CH)],
                         writes=[f"pb{pa}"],
                         note=f"{tg}:wo{oc}")
                act(out_s[:, oc, :], pbank[pa][:, :], AF.Copy,
                    reads=[f"pb{pa}"], writes=["out_s"],
                    note=f"{tg}:woev{oc}")
            emit_ln_residual(xa, xar, f"{tg}:ln3")
            dma("sync", sp_x2[s, :, :, tok], out_s[:, :, :], SX2,
                reads=["out_s"], writes=[f"sp_x2_{s}_{j}"],
                note=f"{tg}:x2dma")
            pace_spill(2)

    finish_spills()

    bar_idx = len(P.events)
    bar_waits = [(CT, CT.n), (CS, CS.n), (CV, CV.n), (CG, CG.n),
                 (SX2, SX2.n)]

    a_stack.close()
    gten = E(nc.sbuf_tensor("gten", [128, HCH, TILE], BF16))
    uten = E(nc.sbuf_tensor("uten", [128, HCH, TILE], BF16))
    hslab = [E(nc.sbuf_tensor(f"hslab{i}", [128, HCH, 128], BF16))
             for i in range(2)]
    ab0 = E(nc.sbuf_tensor("ab0", [128, TILE], BF16))
    ab1 = E(nc.sbuf_tensor("ab1", [128, TILE], BF16))

    NB = SEQ * NT

    def b_parity(n):
        return (gt[0] + n) % 2

    def b_ld_ln(n):
        s2, j2 = n // NT, n % NT
        tok2 = slice(j2 * TILE, (j2 + 1) * TILE)
        gp = b_parity(n)
        xa, xar = x_s[gp], f"x_s{gp}"
        xn, xnf = xn_s[gp], "xn" + str(gp) + "_{}"
        dma("sync", xa[:, :, :], sp_x2[s2, :, :, tok2], SX,
            reads=[f"sp_x2_{s2}_{j2}"], writes=[xar],
            note=f"B{n}:x2ld")
        emit_ln(xa, xar, xn, xnf, f"B{n}:ln2", extra_first=[(SX, SX.n)])

    b_ld_ln(0)

    for n in range(NB):
        s2, j2 = n // NT, n % NT
        gp = b_parity(n)
        xa, xar = x_s[gp], f"x_s{gp}"
        xn, xnf = xn_s[gp], "xn" + str(gp) + "_{}"
        tg = f"B{n}"

        for grp in range(8):
            sl1, sl2 = (2 * grp) % 6, (2 * grp + 1) % 6
            dma("gpsimd", awt[:, :, 512 * sl1:512 * (sl1 + 1)], sp_w1[grp],
                WSL[sl1], reads=[f"sp_w1p{grp // 4}"], writes=[f"wsl{sl1}"],
                note=f"{tg}:w1ld{grp}")
            dma("gpsimd", awt[:, :, 512 * sl2:512 * (sl2 + 1)], sp_w2[grp],
                WSL[sl2], reads=[f"sp_w2p{grp // 4}"], writes=[f"wsl{sl2}"],
                note=f"{tg}:w2ld{grp}")
            r1 = [(WSL[sl1], WSL[sl1].n)]
            r2 = [(WSL[sl2], WSL[sl2].n)]
            for q in range(4):
                oc = 4 * grp + q
                pa, pb2 = 2 * (oc % 2), 2 * (oc % 2) + 1
                mm_group([(awt[:, ic,
                               512 * sl1 + 128 * q:512 * sl1 + 128 * (q + 1)],
                           xn[:, ic, :]) for ic in range(CH)],
                         pbank[pa][:, :],
                         reads=[f"wsl{sl1}"] + [xnf.format(ic)
                                                for ic in range(CH)],
                         writes=[f"pb{pa}"], extra=r1,
                         note=f"{tg}:w1mm{oc}")
                mm_group([(awt[:, ic,
                               512 * sl2 + 128 * q:512 * sl2 + 128 * (q + 1)],
                           xn[:, ic, :]) for ic in range(CH)],
                         pbank[pb2][:, :],
                         reads=[f"wsl{sl2}"] + [xnf.format(ic)
                                                for ic in range(CH)],
                         writes=[f"pb{pb2}"], extra=r2,
                         note=f"{tg}:w2mm{oc}")
                act(ab0[:, :], pbank[pa][:, :], AF.Silu,
                    scale=scl_s["s1v"][:, oc:oc + 1],
                    reads=[f"pb{pa}", "s1v"], writes=["ab0"],
                    note=f"{tg}:silu{oc}")
                act(ab1[:, :], pbank[pb2][:, :], AF.Copy,
                    scale=scl_s["s2v"][:, oc:oc + 1],
                    reads=[f"pb{pb2}", "s2v"], writes=["ab1"],
                    note=f"{tg}:w2ev{oc}")
                tt(gten[:, oc, :], ab0[:, :], ab1[:, :], ALU.mult,
                   reads=["ab0", "ab1"], writes=[f"g{oc}"],
                   note=f"{tg}:g{oc}")

        for oc in range(HCH):
            hb2 = oc % 2
            dma("gpsimd", hslab[hb2][:, :, :], sp_h1[oc], HSL[hb2],
                reads=[f"sp_h1p{oc // 4}"], writes=[f"hslab{hb2}"],
                note=f"{tg}:h1ld{oc}")
            hr = [(HSL[hb2], HSL[hb2].n)]
            pa = oc % 4
            mm_group([(hslab[hb2][:, ic, :], gten[:, ic, :])
                      for ic in range(HCH)],
                     pbank[pa][:, :],
                     reads=[f"hslab{hb2}"] + [f"g{ic}" for ic in range(HCH)],
                     writes=[f"pb{pa}"], extra=hr, note=f"{tg}:h1mm{oc}")
            act(uten[:, oc, :], pbank[pa][:, :], AF.Gelu,
                scale=scl_s["s3v"][:, oc:oc + 1],
                reads=[f"pb{pa}", "s3v"], writes=[f"u{oc}"],
                note=f"{tg}:gelu1_{oc}")

        if n + 1 < NB:
            b_ld_ln(n + 1)

        for oc in range(HCH):
            hb2 = oc % 2
            dma("gpsimd", hslab[hb2][:, :, :], sp_h2[oc], HSL[hb2],
                reads=[f"sp_h2p{oc // 4}"], writes=[f"hslab{hb2}"],
                note=f"{tg}:h2ld{oc}")
            hr = [(HSL[hb2], HSL[hb2].n)]
            pa = oc % 4
            mm_group([(hslab[hb2][:, ic, :], uten[:, ic, :])
                      for ic in range(HCH)],
                     pbank[pa][:, :],
                     reads=[f"hslab{hb2}"] + [f"u{ic}" for ic in range(HCH)],
                     writes=[f"pb{pa}"], extra=hr, note=f"{tg}:h2mm{oc}")
            act(gten[:, oc, :], pbank[pa][:, :], AF.Gelu,
                scale=scl_s["s4v"][:, oc:oc + 1],
                reads=[f"pb{pa}", "s4v"], writes=[f"g{oc}"],
                note=f"{tg}:gelu2_{oc}")

        for oc in range(CH):
            hb2 = oc % 2
            dma("gpsimd", hslab[hb2][:, :, :], sp_w3[oc], HSL[hb2],
                reads=[f"sp_w3p{oc // 4}"], writes=[f"hslab{hb2}"],
                note=f"{tg}:w3ld{oc}")
            hr = [(HSL[hb2], HSL[hb2].n)]
            pa = oc % 4
            mm_group([(hslab[hb2][:, ic, :], gten[:, ic, :])
                      for ic in range(HCH)],
                     pbank[pa][:, :],
                     reads=[f"hslab{hb2}"] + [f"g{ic}" for ic in range(HCH)],
                     writes=[f"pb{pa}"], extra=hr, note=f"{tg}:w3mm{oc}")
            act(out_s[:, oc, :], pbank[pa][:, :], AF.Copy,
                scale=scl_s["s5v"][:, oc:oc + 1],
                reads=[f"pb{pa}", "s5v"], writes=["out_s"],
                note=f"{tg}:w3ev{oc}")

        emit_ln_residual(xa, xar, f"{tg}:ln4")
        dma("sync", dy[:, :, s2 * T + j2 * TILE:s2 * T + (j2 + 1) * TILE],
            out_s[:, :, :], SO, reads=["out_s"], writes=[],
            note=f"{tg}:ydma")

    seen_eng = set()
    for i in range(bar_idx, len(P.events)):
        e = P.events[i]
        if e["eng"] in seen_eng:
            continue
        seen_eng.add(e["eng"])
        own = P.eng_ctr.get(e["eng"])
        merged = {sm: v for sm, v in e["waits"]}
        for sm, v in bar_waits:
            if sm is own or v <= 0:
                continue
            if merged.get(sm, 0) < v:
                merged[sm] = v
        e["waits"] = list(merged.items())

    P.check()

    with nc.Block() as block:
        def run_engine(eng_obj, name):
            seen = {}
            for e2 in P.events:
                if e2["eng"] != name:
                    continue
                for s2_, v in e2["waits"]:
                    if seen.get(s2_, 0) >= v:
                        continue
                    seen[s2_] = v
                    eng_obj.wait_ge(s2_.h, v)
                ins = e2["fn"](eng_obj)
                s2_, _v = e2["inc"]
                ins.then_inc(s2_.h, 16 if s2_.dma else 1)
            if name == "sync":
                eng_obj.wait_ge(SO.h, 16 * SEQ * NT)

        @block.tensor
        def _(t):
            run_engine(t, "tensor")

        @block.scalar
        def _(sc):
            run_engine(sc, "scalar")

        @block.vector
        def _(v):
            run_engine(v, "vector")

        @block.gpsimd
        def _(g):
            run_engine(g, "gpsimd")

        @block.sync
        def _(sy):
            run_engine(sy, "sync")

    stack.close()
    return nc



def kernel(**inputs):
    if "nc" not in _CACHED:
        _CACHED["nc"] = _build()
    nc = _CACHED["nc"]

    w = _pack_weights(inputs)
    consts = _host_consts()
    xs = _pack_x(np.asarray(inputs["L0_attn_x"], np.float32))

    base = {}
    base.update(w)
    base.update(consts)
    in_maps = []
    for cid in range(NCORE):
        m = dict(base)
        m["x"] = xs[cid]
        in_maps.append(m)

    res = run_bass_kernel_spmd(nc, in_maps, core_ids=list(range(NCORE)))
    return _unpack_y([r["y"] for r in res.results])


# version 09
# speedup vs baseline: 1.1452x; 1.0151x over previous
import numpy as np
import ml_dtypes
import concourse.bass as bass
import concourse.mybir as mybir
from concourse.bass_utils import run_bass_kernel_spmd

F32 = mybir.dt.float32
BF16 = mybir.dt.bfloat16
AF = mybir.ActivationFunctionType
ALU = mybir.AluOpType
bfdt = ml_dtypes.bfloat16

B, T, C = 16, 2048, 1024
H, D = 16, 64
HID = 4096
NCORE = 8
SEQ = 2
TILE = 512
NT = T // TILE
CH = C // 128
HCH = HID // 128
KT = T // 128
EPS = 1e-5

_CACHED = {}



def _pack_weights(inp):
    w = {}

    def cat(p1, p2):
        a = np.asarray(p1).reshape(H, 64, C)
        b = np.asarray(p2).reshape(H, 64, C)
        return np.stack([a, b], axis=1).reshape(2 * C, C)

    def to_sb(mat, och):
        t = np.asarray(mat).T.reshape(CH, 128, och * 128).transpose(1, 0, 2)
        return np.ascontiguousarray(t).astype(np.float32)

    w["wkcat"] = to_sb(cat(inp["L0_attn_Wk1"], inp["L0_attn_Wk2"]), 16)
    w["wqcat"] = to_sb(cat(inp["L0_attn_Wq1"], inp["L0_attn_Wq2"]), 16)
    w["wv"] = to_sb(inp["L0_attn_Wv"], 8)
    w["wout"] = to_sb(inp["L0_attn_Wout"], 8)

    def ffn_chunks(mat):
        mat = np.asarray(mat)
        o, i = mat.shape
        noc, nic = o // 128, i // 128
        t = mat.T.reshape(nic, 128, noc, 128).transpose(2, 1, 0, 3)
        return np.ascontiguousarray(t).astype(np.float32)

    def ffn_groups(mat):
        mat = np.asarray(mat)
        t = mat.T.reshape(CH, 128, 8, 512).transpose(2, 1, 0, 3)
        return np.ascontiguousarray(t).astype(np.float32)

    w["w1src"] = ffn_groups(inp["L0_ffn_w1_w"])
    w["w2src"] = ffn_groups(inp["L0_ffn_w2_w"])
    w["h1src"] = ffn_chunks(inp["L0_ffn_h1_w"])
    w["h2src"] = ffn_chunks(inp["L0_ffn_h2_w"])
    w["w3src"] = ffn_chunks(inp["L0_ffn_w3_w"])

    def scl(v):
        return np.ascontiguousarray(np.asarray(v).reshape(-1, 128).T).astype(
            np.float32)

    w["s1v"] = scl(inp["L0_ffn_w1_s"])
    w["s2v"] = scl(inp["L0_ffn_w2_s"])
    w["s3v"] = scl(inp["L0_ffn_h1_s"])
    w["s4v"] = scl(inp["L0_ffn_h2_s"])
    w["s5v"] = scl(inp["L0_ffn_w3_s"])
    return w


def _host_consts():
    c = {}
    inv = 1.0 / (10000.0 ** (np.arange(0, D, 2, dtype=np.float32) / D))
    freqs = np.arange(T, dtype=np.float32)[:, None] * inv[None, :]
    cos = np.cos(freqs).astype(bfdt).astype(np.float32).T
    sin = np.sin(freqs).astype(bfdt).astype(np.float32).T
    c["ctab"] = np.ascontiguousarray(np.tile(cos, (4, 1)))
    c["stab"] = np.ascontiguousarray(
        np.concatenate([-sin, sin, -sin, sin], axis=0))
    m = np.arange(128)
    swp = np.where((m % 64) < 32, m + 32, m - 32)
    S = np.zeros((128, 128), np.float32)
    S[swp, m] = 1.0
    c["swapm"] = S
    c["onesm"] = np.full((128, 128), 1.0 / C, np.float32)
    return c


def _pack_x(x):
    out = []
    for cid in range(NCORE):
        xc = x[2 * cid:2 * cid + 2]
        t = xc.transpose(0, 2, 1).reshape(SEQ, CH, 128, T).transpose(0, 2, 1, 3)
        out.append(np.ascontiguousarray(t).astype(np.float32))
    return out


def _unpack_y(parts):
    outs = []
    for arr in parts:
        a = arr.reshape(128, CH, SEQ, T).transpose(2, 3, 1, 0)
        outs.append(a.reshape(SEQ, T, C))
    return np.ascontiguousarray(np.concatenate(outs, axis=0))



class _Sem:
    __slots__ = ("h", "n", "name", "dma")

    def __init__(self, h, name, dma=False):
        self.h = h
        self.n = 0
        self.name = name
        self.dma = dma


class _Prog:
    def __init__(self):
        self.events = []
        self.writes = {}
        self.guards = {}
        self.eng_ctr = {}

    def ev(self, eng, fn, reads=(), writes=(), extra=(), dma_sem=None, note=""):
        waits = {}

        def add(pair):
            if pair is None:
                return
            s, v = pair
            if v <= 0:
                return
            if waits.get(s, 0) < v:
                waits[s] = v

        for r in reads:
            add(self.writes.get(r))
        for r in writes:
            for p in self.guards.get(r, ()):
                add(p)
            add(self.writes.get(r))
        for p in extra:
            add(p)

        if dma_sem is not None:
            dma_sem.n += 16
            inc = (dma_sem, dma_sem.n)
        else:
            ctr = self.eng_ctr[eng]
            ctr.n += 1
            inc = (ctr, ctr.n)

        own = self.eng_ctr.get(eng)
        wl = [(s, v) for s, v in waits.items() if s is not own]

        self.events.append(dict(eng=eng, fn=fn, waits=wl, inc=inc, note=note))
        for r in reads:
            self.guards.setdefault(r, []).append(inc)
        for r in writes:
            self.writes[r] = inc
            self.guards[r] = []
        return inc

    def check(self):
        eng_names = sorted({e["eng"] for e in self.events})
        queues = {n: [e for e in self.events if e["eng"] == n]
                  for n in eng_names}
        ptr = {n: 0 for n in eng_names}
        val = {}
        total = len(self.events)
        done = 0
        progress = True
        while progress:
            progress = False
            for n in eng_names:
                while ptr[n] < len(queues[n]):
                    e = queues[n][ptr[n]]
                    if all(val.get(s, 0) >= v for s, v in e["waits"]):
                        s, v = e["inc"]
                        val[s] = max(val.get(s, 0), v)
                        ptr[n] += 1
                        done += 1
                        progress = True
                    else:
                        break
        if done != total:
            lines = []
            for n in eng_names:
                if ptr[n] < len(queues[n]):
                    e = queues[n][ptr[n]]
                    missing = [(s.name, v, val.get(s, 0))
                               for s, v in e["waits"] if val.get(s, 0) < v]
                    lines.append(f"  {n}@{ptr[n]}/{len(queues[n])} "
                                 f"'{e['note']}' missing {missing}")
            raise RuntimeError("schedule deadlock:\n" + "\n".join(lines))



def _build(dbg=False):
    from contextlib import ExitStack

    nc = bass.Bass("TRN2", target_bir_lowering=False, debug=False)
    P = _Prog()

    dx = nc.dram_tensor("x", [SEQ, 128, CH, T], F32, kind="ExternalInput").ap()
    dwk = nc.dram_tensor("wkcat", [128, CH, 2048], F32, kind="ExternalInput").ap()
    dwq = nc.dram_tensor("wqcat", [128, CH, 2048], F32, kind="ExternalInput").ap()
    dwv = nc.dram_tensor("wv", [128, CH, 1024], F32, kind="ExternalInput").ap()
    dwo = nc.dram_tensor("wout", [128, CH, 1024], F32, kind="ExternalInput").ap()
    dct = nc.dram_tensor("ctab", [128, T], F32, kind="ExternalInput").ap()
    dstb = nc.dram_tensor("stab", [128, T], F32, kind="ExternalInput").ap()
    dsw = nc.dram_tensor("swapm", [128, 128], F32, kind="ExternalInput").ap()
    don = nc.dram_tensor("onesm", [128, 128], F32, kind="ExternalInput").ap()
    dw1 = nc.dram_tensor("w1src", [8, 128, CH, 512], F32, kind="ExternalInput").ap()
    dw2 = nc.dram_tensor("w2src", [8, 128, CH, 512], F32, kind="ExternalInput").ap()
    dh1 = nc.dram_tensor("h1src", [HCH, 128, HCH, 128], F32, kind="ExternalInput").ap()
    dh2 = nc.dram_tensor("h2src", [HCH, 128, HCH, 128], F32, kind="ExternalInput").ap()
    dw3 = nc.dram_tensor("w3src", [CH, 128, HCH, 128], F32, kind="ExternalInput").ap()
    dscl = {}
    for nme, cols in (("s1v", HCH), ("s2v", HCH), ("s3v", HCH), ("s4v", HCH),
                      ("s5v", CH)):
        dscl[nme] = nc.dram_tensor(nme, [128, cols], F32,
                                   kind="ExternalInput").ap()
    dy = nc.dram_tensor("y", [128, CH, SEQ * T], F32, kind="ExternalOutput").ap()

    sp_wk = nc.dram_tensor("sp_wk", [128, CH, 2048], BF16, kind="Internal").ap()
    sp_wq = nc.dram_tensor("sp_wq", [128, CH, 2048], BF16, kind="Internal").ap()
    sp_wv = nc.dram_tensor("sp_wv", [128, CH, 1024], BF16, kind="Internal").ap()
    sp_wo = nc.dram_tensor("sp_wo", [128, CH, 1024], BF16, kind="Internal").ap()
    sp_w1 = nc.dram_tensor("sp_w1", [8, 128, CH, 512], BF16, kind="Internal").ap()
    sp_w2 = nc.dram_tensor("sp_w2", [8, 128, CH, 512], BF16, kind="Internal").ap()
    sp_h1 = nc.dram_tensor("sp_h1", [HCH, 128, HCH, 128], BF16, kind="Internal").ap()
    sp_h2 = nc.dram_tensor("sp_h2", [HCH, 128, HCH, 128], BF16, kind="Internal").ap()
    sp_w3 = nc.dram_tensor("sp_w3", [CH, 128, HCH, 128], BF16, kind="Internal").ap()
    kdbg = "ExternalOutput" if dbg else "Internal"
    sp_k = nc.dram_tensor("sp_k", [NT, 128, 4, 16, 128], BF16, kind=kdbg).ap()
    sp_v = nc.dram_tensor("sp_v", [KT, 128, 1024], BF16, kind=kdbg).ap()
    sp_x2 = nc.dram_tensor("sp_x2", [SEQ, 128, CH, T], F32,
                           kind="ExternalOutput" if dbg else "Internal").ap()

    stack = ExitStack()
    E = stack.enter_context

    def sem(name, dma=False):
        return _Sem(E(nc.semaphore(name)), name, dma)

    SW = sem("SW", True)
    SC = sem("SC", True)
    SPL = sem("SPL", True)
    SX = sem("SX", True)
    SKK = sem("SKK", True)
    SKV = [sem("SKV0", True), sem("SKV1", True)]
    KVS = [sem(f"KVS{i}", True) for i in range(4)]
    HSL = [sem("HSL0", True), sem("HSL1", True)]
    WSL = [sem(f"WSL{i}", True) for i in range(6)]
    SX2 = sem("SX2", True)
    SO = sem("SO", True)
    CT = sem("CT")
    CS = sem("CS")
    CV = sem("CV")
    CG = sem("CG")
    P.eng_ctr = {"tensor": CT, "scalar": CS, "vector": CV, "gpsimd": CG}

    awt = E(nc.sbuf_tensor("awt", [128, CH, 3072], BF16))
    x_s = [E(nc.sbuf_tensor(f"x_s{i}", [128, CH, TILE], F32)) for i in range(2)]
    xn_s = [E(nc.sbuf_tensor(f"xn_s{i}", [128, CH, TILE], BF16))
            for i in range(2)]
    cbf = [E(nc.sbuf_tensor(f"cbf{i}", [128, TILE], BF16)) for i in range(2)]
    csq = [E(nc.sbuf_tensor(f"csq{i}", [128, TILE], BF16)) for i in range(2)]
    mean_s = E(nc.sbuf_tensor("mean_s", [128, TILE], F32))
    rstd_s = E(nc.sbuf_tensor("rstd_s", [128, TILE], F32))
    vtmp = E(nc.sbuf_tensor("vtmp", [128, TILE], F32))
    t1_s = E(nc.sbuf_tensor("t1_s", [128, TILE], F32))
    out_s = E(nc.sbuf_tensor("out_s", [128, CH, TILE], F32))
    swap_s = E(nc.sbuf_tensor("swap_s", [128, 128], BF16))
    ones_s = E(nc.sbuf_tensor("ones_s", [128, 128], BF16))
    epsb = E(nc.sbuf_tensor("epsb", [128, 1], F32))
    scl_s = {}
    for nme, cols in (("s1v", HCH), ("s2v", HCH), ("s3v", HCH), ("s4v", HCH),
                      ("s5v", CH)):
        scl_s[nme] = E(nc.sbuf_tensor("sb_" + nme, [128, cols], F32))

    a_stack = ExitStack()
    A_ = a_stack.enter_context
    t2_s = A_(nc.sbuf_tensor("t2_s", [128, TILE], BF16))
    ctab_s = A_(nc.sbuf_tensor("ctab_s", [128, T], F32))
    stab_s = A_(nc.sbuf_tensor("stab_s", [128, T], F32))
    k12st = A_(nc.sbuf_tensor("k12st", [128, 4, 16, 128], BF16))
    vstage = [A_(nc.sbuf_tensor(f"vstage{i}", [128, 1024], BF16))
              for i in range(2)]
    q12_s = A_(nc.sbuf_tensor("q12_s", [128, 16, TILE], BF16))
    kslab = [A_(nc.sbuf_tensor(f"kslab{i}", [128, 8, 128], BF16))
             for i in range(4)]
    vslab = [A_(nc.sbuf_tensor(f"vslab{i}", [128, TILE], BF16))
             for i in range(4)]
    s1s = A_(nc.sbuf_tensor("s1s", [128, TILE], F32))
    p_s = A_(nc.sbuf_tensor("p_s", [128, 2, TILE], BF16))
    yfm = A_(nc.sbuf_tensor("yfm", [128, CH, TILE], BF16))

    pair0 = E(nc.psum_tensor("pair0", [128, 1024], F32))
    pair1 = E(nc.psum_tensor("pair1", [128, 1024], F32))
    pbank = [E(nc.psum_tensor(f"pb{i}", [128, 512], F32)) for i in range(4)]

    def dma(eng, out_ap, in_ap, s, reads=(), writes=(), extra=(), note=""):
        def fn(e, o=out_ap, i=in_ap):
            return e.dma_start(out=o, in_=i)
        return P.ev(eng, fn, reads=reads, writes=writes, extra=extra,
                    dma_sem=s, note=note)

    def mm_group(pairs, out_ap, reads, writes, extra=(), note="", start=True,
                 stop=True, skip=False):
        def fn(t, pr=tuple(pairs), o=out_ap, st=start, sp=stop, sk=skip):
            ins = None
            n = len(pr)
            for i, (l, r) in enumerate(pr):
                ins = t.matmul(o, l, r, start=(st and i == 0),
                               stop=(sp and i == n - 1), skip_group_check=sk)
            return ins
        return P.ev("tensor", fn, reads=reads, writes=writes, extra=extra,
                    note=note)

    def act(out_ap, in_ap, func, reads, writes, scale=1.0, bias=0.0, extra=(),
            note=""):
        def fn(s, o=out_ap, i=in_ap, f=func, sc=scale, b=bias):
            return s.activation(o, i, f, scale=sc, bias=b)
        return P.ev("scalar", fn, reads=reads, writes=writes, extra=extra,
                    note=note)

    def tt(out_ap, a_ap, b_ap, op, reads, writes, extra=(), note=""):
        def fn(v, o=out_ap, a=a_ap, b=b_ap, p=op):
            return v.tensor_tensor(o, a, b, p)
        return P.ev("vector", fn, reads=reads, writes=writes, extra=extra,
                    note=note)

    def vcopy(out_ap, in_ap, reads, writes, extra=(), note=""):
        def fn(v, o=out_ap, i=in_ap):
            return v.tensor_copy(o, i)
        return P.ev("vector", fn, reads=reads, writes=writes, extra=extra,
                    note=note)

    def vrecip(out_ap, in_ap, reads, writes, note=""):
        def fn(v, o=out_ap, i=in_ap):
            return v.reciprocal(o, i)
        return P.ev("vector", fn, reads=reads, writes=writes, note=note)

    def r3(ap):
        return ap.rearrange("p (m t) -> p m t", m=4)

    def emit_ln_stats(src, src_res, tag, extra_first=()):
        for c in range(CH):
            pp = c % 2
            vcopy(cbf[pp][:, :], src[:, c, :],
                  reads=[src_res], writes=[f"cbf{pp}"],
                  extra=extra_first if c == 0 else (),
                  note=f"{tag}:cast{c}")
            tt(csq[pp][:, :], cbf[pp][:, :], cbf[pp][:, :], ALU.mult,
               reads=[f"cbf{pp}"], writes=[f"csq{pp}"], note=f"{tag}:sq{c}")
            mm_group([(ones_s[:, :], cbf[pp][:, :])], pbank[0][:, :],
                     reads=[f"cbf{pp}", "ones_s"],
                     writes=["pb0"] if c == 0 else [],
                     extra=() if c == 0 else [P.writes.get("pb0")],
                     note=f"{tag}:mmm{c}", start=(c == 0), stop=(c == CH - 1))
            if c != 0:
                P.writes["pb0"] = P.events[-1]["inc"]
            mm_group([(ones_s[:, :], csq[pp][:, :])], pbank[1][:, :],
                     reads=[f"csq{pp}", "ones_s"],
                     writes=["pb1"] if c == 0 else [],
                     extra=() if c == 0 else [P.writes.get("pb1")],
                     note=f"{tag}:mms{c}", start=(c == 0), stop=(c == CH - 1))
            if c != 0:
                P.writes["pb1"] = P.events[-1]["inc"]
        act(mean_s[:, :], pbank[0][:, :], AF.Copy, reads=["pb0"],
            writes=["mean_s"], note=f"{tag}:mean_ev")
        tt(vtmp[:, :], mean_s[:, :], mean_s[:, :], ALU.mult,
           reads=["mean_s"], writes=["vtmp"], note=f"{tag}:mean2")
        tt(vtmp[:, :], pbank[1][:, :], vtmp[:, :], ALU.subtract,
           reads=["pb1"], writes=["vtmp"], note=f"{tag}:var")
        act(t1_s[:, :], vtmp[:, :], AF.Sqrt, scale=1.0, bias=epsb[:, 0:1],
            reads=["vtmp", "epsb"], writes=["t1_s"], note=f"{tag}:sqrt")
        vrecip(rstd_s[:, :], t1_s[:, :], reads=["t1_s"], writes=["rstd_s"],
               note=f"{tag}:recip")

    def emit_ln(src, src_res, xn, xn_fmt, tag, extra_first=()):
        emit_ln_stats(src, src_res, tag, extra_first)
        for c in range(CH):
            tt(t1_s[:, :], src[:, c, :], mean_s[:, :], ALU.subtract,
               reads=[src_res, "mean_s"], writes=["t1_s"], note=f"{tag}:sub{c}")
            tt(xn[:, c, :], t1_s[:, :], rstd_s[:, :], ALU.mult,
               reads=["t1_s", "rstd_s"], writes=[xn_fmt.format(c)],
               note=f"{tag}:nrm{c}")

    def emit_ln_residual(xr, xr_res, tag):
        emit_ln_stats(out_s, "out_s", tag)
        for c in range(CH):
            tt(t1_s[:, :], out_s[:, c, :], mean_s[:, :], ALU.subtract,
               reads=["out_s", "mean_s"], writes=["t1_s"], note=f"{tag}:sub{c}")
            tt(vtmp[:, :], t1_s[:, :], rstd_s[:, :], ALU.mult,
               reads=["t1_s", "rstd_s"], writes=["vtmp"], note=f"{tag}:nrm{c}")
            tt(out_s[:, c, :], vtmp[:, :], xr[:, c, :], ALU.add,
               reads=["vtmp", xr_res], writes=["out_s"], note=f"{tag}:res{c}")

    def emit_rope(dst_fn, wrap3, xn, xn_fmt, ct_sl, st_sl, tag):
        def proj(c):
            pa = c % 2
            mm_group([(awt[:, ic, 128 * c:128 * (c + 1)], xn[:, ic, :])
                      for ic in range(CH)],
                     pbank[pa][:, :],
                     reads=[f"awtA{c // 4}"] + [xn_fmt.format(ic)
                                               for ic in range(CH)],
                     writes=[f"pb{pa}"], note=f"{tag}:proj{c}")

        def tail(c):
            pa, pb = c % 2, 2 + c % 2
            tt(t1_s[:, :], pbank[pa][:, :], ct_sl, ALU.mult,
               reads=[f"pb{pa}", "ctab"], writes=["t1_s"], note=f"{tag}:ct{c}")
            tt(t2_s[:, :], pbank[pa][:, :], st_sl, ALU.mult,
               reads=[f"pb{pa}", "stab"], writes=["t2_s"], note=f"{tag}:st{c}")
            mm_group([(swap_s[:, :], t2_s[:, :])], pbank[pb][:, :],
                     reads=["t2_s", "swap_s"], writes=[f"pb{pb}"],
                     note=f"{tag}:swap{c}")
            dap, dres = dst_fn(c)
            if wrap3:
                tt(dap, r3(pbank[pb][:, :]), r3(t1_s[:, :]), ALU.add,
                   reads=[f"pb{pb}", "t1_s"], writes=[dres],
                   note=f"{tag}:rot{c}")
            else:
                tt(dap, pbank[pb][:, :], t1_s[:, :], ALU.add,
                   reads=[f"pb{pb}", "t1_s"], writes=[dres],
                   note=f"{tag}:rot{c}")

        for c in range(17):
            if c < 16:
                proj(c)
            if c >= 1:
                tail(c - 1)

    def msfn(v, o=epsb[:, 0:1], cc=EPS):
        return v.memset(o, cc)
    P.ev("vector", msfn, writes=["epsb"], note="memset eps")

    dma("gpsimd", ones_s[:, :], don, SC, writes=["ones_s"], note="ld ones")
    dma("gpsimd", swap_s[:, :], dsw, SC, writes=["swap_s"], note="ld swap")
    dma("scalar", ctab_s[:, :], dct, SC, writes=["ctab"], note="ld ctab")
    dma("scalar", stab_s[:, :], dstb, SC, writes=["stab"], note="ld stab")
    for nme in scl_s:
        dma("scalar", scl_s[nme][:, :], dscl[nme], SC, writes=[nme],
            note=f"ld {nme}")

    for q in range(4):
        dma("gpsimd", awt[:, :, 512 * q:512 * (q + 1)],
            dwk[:, :, 512 * q:512 * (q + 1)], SW,
            writes=[f"awtA{q}"], note=f"wk0:A{q}")
    for hh in range(2):
        dma("gpsimd", awt[:, :, 2048 + 512 * hh:2048 + 512 * (hh + 1)],
            dwv[:, :, 512 * hh:512 * (hh + 1)], SW,
            writes=[f"awtB{hh}"], note=f"wv0:B{hh}")

    spill_pieces = []
    for q in range(4):
        spill_pieces.append((sp_wq[:, :, 512 * q:512 * (q + 1)],
                             dwq[:, :, 512 * q:512 * (q + 1)], f"sp_wqd{q}"))
    for hh in range(2):
        spill_pieces.append((sp_wo[:, :, 512 * hh:512 * (hh + 1)],
                             dwo[:, :, 512 * hh:512 * (hh + 1)], f"sp_wod{hh}"))
    for q in range(4):
        spill_pieces.append((sp_wk[:, :, 512 * q:512 * (q + 1)],
                             dwk[:, :, 512 * q:512 * (q + 1)], f"sp_wkd{q}"))
    for hh in range(2):
        spill_pieces.append((sp_wv[:, :, 512 * hh:512 * (hh + 1)],
                             dwv[:, :, 512 * hh:512 * (hh + 1)], f"sp_wvd{hh}"))
    for g in range(8):
        spill_pieces.append((sp_w1[g], dw1[g], f"sp_w1p{g}"))
        spill_pieces.append((sp_w2[g], dw2[g], f"sp_w2p{g}"))
    for oc in range(8):
        spill_pieces.append((sp_w3[oc], dw3[oc], f"sp_w3p{oc}"))
    for oc in range(HCH):
        spill_pieces.append((sp_h1[oc], dh1[oc], f"sp_h1p{oc}"))
    for oc in range(HCH):
        spill_pieces.append((sp_h2[oc], dh2[oc], f"sp_h2p{oc}"))
    spill_i = [0]

    def pace_spill(k):
        for _ in range(k):
            if spill_i[0] >= len(spill_pieces):
                return
            d2, s2_, res = spill_pieces[spill_i[0]]
            spill_i[0] += 1
            dma("gpsimd", d2, s2_, SPL, writes=[res], extra=[(CT, CT.n)],
                note=f"spill {res}")

    def finish_spills():
        while spill_i[0] < len(spill_pieces):
            pace_spill(4)

    def load_awtA(src, res_fmt, tag):
        for q in range(4):
            dma("sync", awt[:, :, 512 * q:512 * (q + 1)],
                src[:, :, 512 * q:512 * (q + 1)], SW,
                reads=[res_fmt.format(q)], writes=[f"awtA{q}"],
                note=f"{tag}:A{q}")

    def load_awtB(src, res_fmt, tag):
        for hh in range(2):
            dma("sync", awt[:, :, 2048 + 512 * hh:2048 + 512 * (hh + 1)],
                src[:, :, 512 * hh:512 * (hh + 1)], SW,
                reads=[res_fmt.format(hh)], writes=[f"awtB{hh}"],
                note=f"{tag}:B{hh}")

    gt = [0]
    x0_pre = [False]

    def x_load(s2, t2, gp2, tag):
        tok2 = slice(t2 * TILE, (t2 + 1) * TILE)
        dma("scalar", x_s[gp2][:, :, :], dx[s2, :, :, tok2], SX,
            writes=[f"x_s{gp2}"], note=f"{tag}:x")

    for s in range(SEQ):
        if not x0_pre[0]:
            x_load(s, 0, gt[0] % 2, f"A1s{s}t0")
        x0_pre[0] = False

        for it in range(NT):
            tok = slice(it * TILE, (it + 1) * TILE)
            gp = gt[0] % 2
            gt[0] += 1
            xa, xar = x_s[gp], f"x_s{gp}"
            xn, xnf = xn_s[gp], "xn" + str(gp) + "_{}"
            tg = f"A1s{s}t{it}"
            emit_ln(xa, xar, xn, xnf, tg, extra_first=[(SX, SX.n)])
            if it + 1 < NT:
                x_load(s, it + 1, gt[0] % 2, f"A1s{s}t{it + 1}")
            emit_rope(lambda c: (k12st[:, :, c, :], f"k12st{c}"), True,
                      xn, xnf, ctab_s[:, tok], stab_s[:, tok], tg)
            if it == NT - 1:
                load_awtA(sp_wq, "sp_wqd{}", f"A2s{s}wq")
            for m in range(4):
                vb = m % 2
                vpair = pair0 if m % 2 == 0 else pair1
                vres = "pair0" if m % 2 == 0 else "pair1"
                for half in range(2):
                    mm_group(
                        [(xn[:, ic, 128 * m:128 * (m + 1)],
                          awt[:, ic, 2048 + 512 * half:2048 + 512 * (half + 1)])
                         for ic in range(CH)],
                        vpair[:, 512 * half:512 * (half + 1)],
                        reads=[xnf.format(ic) for ic in range(CH)]
                              + [f"awtB{half}"],
                        writes=[f"{vres}{'ab'[half]}"],
                        note=f"{tg}:v{m}h{half}")
                act(vstage[vb][:, :], vpair[:, :], AF.Copy,
                    reads=[f"{vres}a", f"{vres}b"], writes=[f"vstage{vb}"],
                    note=f"{tg}:vev{m}")
                dma("sync", sp_v[4 * it + m], vstage[vb][:, :], SKV[vb],
                    reads=[f"vstage{vb}"], writes=[f"sp_v{4 * it + m}"],
                    note=f"{tg}:vdma{m}")
            if it == NT - 1:
                load_awtB(sp_wo, "sp_wod{}", f"A2s{s}wo")
            dma("sync", sp_k[it], k12st[:, :, :, :], SKK,
                reads=[f"k12st{c}" for c in range(16)],
                writes=[f"sp_kt{it}"], note=f"{tg}:kdump")
            pace_spill(6)

        a2_x = {}

        def a2_prep(j):
            tok = slice(j * TILE, (j + 1) * TILE)
            gp = gt[0] % 2
            gt[0] += 1
            xa, xar = x_s[gp], f"x_s{gp}"
            xn, xnf = xn_s[gp], "xn" + str(gp) + "_{}"
            a2_x[j] = (xa, xar)
            tg = f"A2s{s}q{j}"
            x_load(s, j, gp, tg)
            emit_ln(xa, xar, xn, xnf, tg, extra_first=[(SX, SX.n)])
            emit_rope(lambda c: (q12_s[:, c, :], f"q12_{c}"), False,
                      xn, xnf, ctab_s[:, tok], stab_s[:, tok], tg)

        a2_prep(0)

        cons = [(j2, hb2, kt2) for j2 in range(NT) for hb2 in range(2)
                for kt2 in range(4 * (j2 + 1))]
        k_rdy = {}
        next_load = [0]

        def emit_slab_load(ci):
            j2, hb2, kt2 = cons[ci]
            bu = ci % 4
            it2, m2 = divmod(kt2, 4)
            dma("sync", kslab[bu][:, :, :],
                sp_k[it2, :, m2, 8 * hb2:8 * (hb2 + 1), :], KVS[bu],
                reads=[f"sp_kt{it2}"], writes=[f"kslab{bu}"],
                note=f"s{s}ci{ci}:kld")
            dma("sync", vslab[bu][:, :],
                sp_v[kt2][:, 512 * hb2:512 * (hb2 + 1)], KVS[bu],
                reads=[f"sp_v{kt2}"], writes=[f"vslab{bu}"],
                note=f"s{s}ci{ci}:vld")
            k_rdy[ci] = [(KVS[bu], KVS[bu].n)]

        def pump(ci_cur):
            while next_load[0] < len(cons) and next_load[0] <= ci_cur + 3:
                emit_slab_load(next_load[0])
                next_load[0] += 1

        pump(-1)
        ci_base = 0
        for j in range(NT):
            tok = slice(j * TILE, (j + 1) * TILE)
            nkt = 4 * (j + 1)
            xa, xar = a2_x[j]
            tg = f"A2s{s}q{j}"

            for hb in range(2):
                iters = [(kt_i, hh) for kt_i in range(nkt) for hh in range(8)]
                n_it = len(iters)
                base = ci_base

                def emit_S(k):
                    kt_i, hh = iters[k]
                    ci = base + kt_i
                    bu = ci % 4
                    h = 8 * hb + hh
                    sa, sb = 2 * (k % 2), 2 * (k % 2) + 1
                    mm_group([(kslab[bu][0:64, hh, :], q12_s[0:64, h, :])],
                             pbank[sa][:, :],
                             reads=[f"kslab{bu}", f"q12_{h}"],
                             writes=[f"pb{sa}"], extra=k_rdy[ci],
                             note=f"{tg}b{hb}k{kt_i}h{hh}:S1")
                    mm_group([(kslab[bu][64:128, hh, :],
                               q12_s[64:128, h, :])],
                             pbank[sb][:, :],
                             reads=[f"kslab{bu}", f"q12_{h}"],
                             writes=[f"pb{sb}"],
                             note=f"{tg}b{hb}k{kt_i}h{hh}:S2")

                def emit_chain(k):
                    kt_i, hh = iters[k]
                    ci = base + kt_i
                    bu = ci % 4
                    sa, sb = 2 * (k % 2), 2 * (k % 2) + 1
                    pbuf = k % 2
                    diag = kt_i >= 4 * j
                    act(s1s[:, :], pbank[sa][:, :], AF.Copy,
                        scale=1.0 / 4096.0,
                        reads=[f"pb{sa}"], writes=["s1s"],
                        note=f"{tg}b{hb}k{kt_i}h{hh}:ev")
                    tt(p_s[:, pbuf, :], pbank[sb][:, :], s1s[:, :],
                       ALU.mult, reads=[f"pb{sb}", "s1s"],
                       writes=[f"p{pbuf}"],
                       note=f"{tg}b{hb}k{kt_i}h{hh}:P")
                    if diag:
                        moff = kt_i - 4 * j

                        def mfn(g, o=p_s[:, pbuf, :], base2=-128 * moff):
                            return g.affine_select(
                                out=o, in_=o, pattern=[[1, TILE]],
                                compare_op=ALU.is_ge, fill=0.0,
                                base=base2, channel_multiplier=-1)
                        P.ev("gpsimd", mfn, reads=[f"p{pbuf}"],
                             writes=[f"p{pbuf}"],
                             note=f"{tg}b{hb}k{kt_i}h{hh}:mask")
                    hh2 = iters[k][1]
                    pt = pair0 if hh2 < 4 else pair1
                    colh = (hh2 % 4) // 2
                    row0 = 64 * (hh2 % 2)
                    yres = f"pair{hh2 // 4}{'ab'[colh]}"
                    mm_group(
                        [(vslab[bu][:, 64 * hh2:64 * hh2 + 64],
                          p_s[:, pbuf, :])],
                        pt[row0:row0 + 64, 512 * colh:512 * (colh + 1)],
                        reads=[f"p{pbuf}", f"vslab{bu}"],
                        writes=[yres], extra=k_rdy[ci],
                        note=f"{tg}b{hb}k{kt_i}h{hh2}:PV",
                        start=(kt_i == 0), stop=(kt_i == nkt - 1),
                        skip=True)
                    if hh2 == 7:
                        pump(ci + 1)

                for k in range(n_it + 2):
                    if k >= 2:
                        emit_chain(k - 2)
                    if k < n_it:
                        emit_S(k)

                for hh in range(8):
                    h = 8 * hb + hh
                    pt = pair0 if hh < 4 else pair1
                    colh = (hh % 4) // 2
                    row0 = 64 * (hh % 2)
                    yres = f"pair{hh // 4}{'ab'[colh]}"

                    def yfn(sc, o=yfm[row0:row0 + 64, h // 2, :],
                            i=pt[row0:row0 + 64, 512 * colh:512 * (colh + 1)]):
                        return sc.copy(o, i)
                    P.ev("scalar", yfn, reads=[yres],
                         writes=[f"yfm{h // 2}_{row0}"],
                         note=f"{tg}b{hb}:yev{hh}")
                ci_base += nkt

            if j + 1 < NT:
                a2_prep(j + 1)
                if j + 1 == NT - 1 and s + 1 < SEQ:
                    load_awtA(sp_wk, "sp_wkd{}", f"A1s{s + 1}wk")
            elif s + 1 < SEQ:
                x_load(s + 1, 0, gt[0] % 2, f"A1s{s + 1}t0")
                x0_pre[0] = True

            for oc in range(CH):
                pa = oc % 4
                mm_group([(awt[:, dc, 2048 + 128 * oc:2048 + 128 * (oc + 1)],
                           yfm[:, dc, :]) for dc in range(CH)],
                         pbank[pa][:, :],
                         reads=[f"awtB{oc // 4}"]
                               + [f"yfm{dc}_0" for dc in range(CH)]
                               + [f"yfm{dc}_64" for dc in range(CH)],
                         writes=[f"pb{pa}"],
                         note=f"{tg}:wo{oc}")
                act(out_s[:, oc, :], pbank[pa][:, :], AF.Copy,
                    reads=[f"pb{pa}"], writes=["out_s"],
                    note=f"{tg}:woev{oc}")
            if j == NT - 1 and s + 1 < SEQ:
                load_awtB(sp_wv, "sp_wvd{}", f"A1s{s + 1}wv")
            emit_ln_residual(xa, xar, f"{tg}:ln3")
            dma("sync", sp_x2[s, :, :, tok], out_s[:, :, :], SX2,
                reads=["out_s"], writes=[f"sp_x2_{s}_{j}"],
                note=f"{tg}:x2dma")
            pace_spill(7)

    finish_spills()

    bar_idx = len(P.events)
    bar_waits = [(CT, CT.n), (CS, CS.n), (CV, CV.n), (CG, CG.n),
                 (SX2, SX2.n)]

    a_stack.close()
    gten = E(nc.sbuf_tensor("gten", [128, HCH, TILE], BF16))
    uten = E(nc.sbuf_tensor("uten", [128, HCH, TILE], BF16))
    hslab = [E(nc.sbuf_tensor(f"hslab{i}", [128, HCH, 128], BF16))
             for i in range(2)]
    ab0 = E(nc.sbuf_tensor("ab0", [128, TILE], BF16))
    ab1 = E(nc.sbuf_tensor("ab1", [128, TILE], BF16))

    NB = SEQ * NT

    def b_parity(n):
        return (gt[0] + n) % 2

    def b_ld_ln(n):
        s2, j2 = n // NT, n % NT
        tok2 = slice(j2 * TILE, (j2 + 1) * TILE)
        gp = b_parity(n)
        xa, xar = x_s[gp], f"x_s{gp}"
        xn, xnf = xn_s[gp], "xn" + str(gp) + "_{}"
        dma("sync", xa[:, :, :], sp_x2[s2, :, :, tok2], SX,
            reads=[f"sp_x2_{s2}_{j2}"], writes=[xar],
            note=f"B{n}:x2ld")
        emit_ln(xa, xar, xn, xnf, f"B{n}:ln2", extra_first=[(SX, SX.n)])

    b_ld_ln(0)

    for n in range(NB):
        s2, j2 = n // NT, n % NT
        gp = b_parity(n)
        xa, xar = x_s[gp], f"x_s{gp}"
        xn, xnf = xn_s[gp], "xn" + str(gp) + "_{}"
        tg = f"B{n}"

        for grp in range(8):
            sl1, sl2 = (2 * grp) % 6, (2 * grp + 1) % 6
            dma("gpsimd", awt[:, :, 512 * sl1:512 * (sl1 + 1)], sp_w1[grp],
                WSL[sl1], reads=[f"sp_w1p{grp}"], writes=[f"wsl{sl1}"],
                note=f"{tg}:w1ld{grp}")
            dma("gpsimd", awt[:, :, 512 * sl2:512 * (sl2 + 1)], sp_w2[grp],
                WSL[sl2], reads=[f"sp_w2p{grp}"], writes=[f"wsl{sl2}"],
                note=f"{tg}:w2ld{grp}")
            r1 = [(WSL[sl1], WSL[sl1].n)]
            r2 = [(WSL[sl2], WSL[sl2].n)]
            for q in range(4):
                oc = 4 * grp + q
                pa, pb2 = 2 * (oc % 2), 2 * (oc % 2) + 1
                mm_group([(awt[:, ic,
                               512 * sl1 + 128 * q:512 * sl1 + 128 * (q + 1)],
                           xn[:, ic, :]) for ic in range(CH)],
                         pbank[pa][:, :],
                         reads=[f"wsl{sl1}"] + [xnf.format(ic)
                                                for ic in range(CH)],
                         writes=[f"pb{pa}"], extra=r1,
                         note=f"{tg}:w1mm{oc}")
                mm_group([(awt[:, ic,
                               512 * sl2 + 128 * q:512 * sl2 + 128 * (q + 1)],
                           xn[:, ic, :]) for ic in range(CH)],
                         pbank[pb2][:, :],
                         reads=[f"wsl{sl2}"] + [xnf.format(ic)
                                                for ic in range(CH)],
                         writes=[f"pb{pb2}"], extra=r2,
                         note=f"{tg}:w2mm{oc}")
                act(ab0[:, :], pbank[pa][:, :], AF.Silu,
                    scale=scl_s["s1v"][:, oc:oc + 1],
                    reads=[f"pb{pa}", "s1v"], writes=["ab0"],
                    note=f"{tg}:silu{oc}")
                act(ab1[:, :], pbank[pb2][:, :], AF.Copy,
                    scale=scl_s["s2v"][:, oc:oc + 1],
                    reads=[f"pb{pb2}", "s2v"], writes=["ab1"],
                    note=f"{tg}:w2ev{oc}")
                tt(gten[:, oc, :], ab0[:, :], ab1[:, :], ALU.mult,
                   reads=["ab0", "ab1"], writes=[f"g{oc}"],
                   note=f"{tg}:g{oc}")

        for oc in range(HCH):
            hb2 = oc % 2
            dma("gpsimd", hslab[hb2][:, :, :], sp_h1[oc], HSL[hb2],
                reads=[f"sp_h1p{oc}"], writes=[f"hslab{hb2}"],
                note=f"{tg}:h1ld{oc}")
            hr = [(HSL[hb2], HSL[hb2].n)]
            pa = oc % 4
            mm_group([(hslab[hb2][:, ic, :], gten[:, ic, :])
                      for ic in range(HCH)],
                     pbank[pa][:, :],
                     reads=[f"hslab{hb2}"] + [f"g{ic}" for ic in range(HCH)],
                     writes=[f"pb{pa}"], extra=hr, note=f"{tg}:h1mm{oc}")
            act(uten[:, oc, :], pbank[pa][:, :], AF.Gelu,
                scale=scl_s["s3v"][:, oc:oc + 1],
                reads=[f"pb{pa}", "s3v"], writes=[f"u{oc}"],
                note=f"{tg}:gelu1_{oc}")

        if n + 1 < NB:
            b_ld_ln(n + 1)

        for oc in range(HCH):
            hb2 = oc % 2
            dma("gpsimd", hslab[hb2][:, :, :], sp_h2[oc], HSL[hb2],
                reads=[f"sp_h2p{oc}"], writes=[f"hslab{hb2}"],
                note=f"{tg}:h2ld{oc}")
            hr = [(HSL[hb2], HSL[hb2].n)]
            pa = oc % 4
            mm_group([(hslab[hb2][:, ic, :], uten[:, ic, :])
                      for ic in range(HCH)],
                     pbank[pa][:, :],
                     reads=[f"hslab{hb2}"] + [f"u{ic}" for ic in range(HCH)],
                     writes=[f"pb{pa}"], extra=hr, note=f"{tg}:h2mm{oc}")
            act(gten[:, oc, :], pbank[pa][:, :], AF.Gelu,
                scale=scl_s["s4v"][:, oc:oc + 1],
                reads=[f"pb{pa}", "s4v"], writes=[f"g{oc}"],
                note=f"{tg}:gelu2_{oc}")

        for oc in range(CH):
            hb2 = oc % 2
            dma("gpsimd", hslab[hb2][:, :, :], sp_w3[oc], HSL[hb2],
                reads=[f"sp_w3p{oc}"], writes=[f"hslab{hb2}"],
                note=f"{tg}:w3ld{oc}")
            hr = [(HSL[hb2], HSL[hb2].n)]
            pa = oc % 4
            mm_group([(hslab[hb2][:, ic, :], gten[:, ic, :])
                      for ic in range(HCH)],
                     pbank[pa][:, :],
                     reads=[f"hslab{hb2}"] + [f"g{ic}" for ic in range(HCH)],
                     writes=[f"pb{pa}"], extra=hr, note=f"{tg}:w3mm{oc}")
            act(out_s[:, oc, :], pbank[pa][:, :], AF.Copy,
                scale=scl_s["s5v"][:, oc:oc + 1],
                reads=[f"pb{pa}", "s5v"], writes=["out_s"],
                note=f"{tg}:w3ev{oc}")

        emit_ln_residual(xa, xar, f"{tg}:ln4")
        dma("sync", dy[:, :, s2 * T + j2 * TILE:s2 * T + (j2 + 1) * TILE],
            out_s[:, :, :], SO, reads=["out_s"], writes=[],
            note=f"{tg}:ydma")

    seen_eng = set()
    for i in range(bar_idx, len(P.events)):
        e = P.events[i]
        if e["eng"] in seen_eng:
            continue
        seen_eng.add(e["eng"])
        own = P.eng_ctr.get(e["eng"])
        merged = {sm: v for sm, v in e["waits"]}
        for sm, v in bar_waits:
            if sm is own or v <= 0:
                continue
            if merged.get(sm, 0) < v:
                merged[sm] = v
        e["waits"] = list(merged.items())

    P.check()

    with nc.Block() as block:
        def run_engine(eng_obj, name):
            seen = {}
            for e2 in P.events:
                if e2["eng"] != name:
                    continue
                for s2_, v in e2["waits"]:
                    if seen.get(s2_, 0) >= v:
                        continue
                    seen[s2_] = v
                    eng_obj.wait_ge(s2_.h, v)
                ins = e2["fn"](eng_obj)
                s2_, _v = e2["inc"]
                ins.then_inc(s2_.h, 16 if s2_.dma else 1)
            if name == "sync":
                eng_obj.wait_ge(SO.h, 16 * SEQ * NT)

        @block.tensor
        def _(t):
            run_engine(t, "tensor")

        @block.scalar
        def _(sc):
            run_engine(sc, "scalar")

        @block.vector
        def _(v):
            run_engine(v, "vector")

        @block.gpsimd
        def _(g):
            run_engine(g, "gpsimd")

        @block.sync
        def _(sy):
            run_engine(sy, "sync")

    stack.close()
    return nc



def kernel(**inputs):
    if "nc" not in _CACHED:
        _CACHED["nc"] = _build()
    nc = _CACHED["nc"]

    w = _pack_weights(inputs)
    consts = _host_consts()
    xs = _pack_x(np.asarray(inputs["L0_attn_x"], np.float32))

    base = {}
    base.update(w)
    base.update(consts)
    in_maps = []
    for cid in range(NCORE):
        m = dict(base)
        m["x"] = xs[cid]
        in_maps.append(m)

    res = run_bass_kernel_spmd(nc, in_maps, core_ids=list(range(NCORE)))
    return _unpack_y([r["y"] for r in res.results])
